# version 02
# baseline (speedup 1.0000x reference)
import math
from contextlib import ExitStack

import numpy as np
import ml_dtypes

import concourse.bass as bass
import concourse.mybir as mybir
from concourse.bass_utils import run_bass_kernel_spmd

F32 = mybir.dt.float32
BF16 = mybir.dt.bfloat16
AF = mybir.ActivationFunctionType
ALU = mybir.AluOpType
AX = mybir.AxisListType

T = 4096
D = 4096
DH = 128
NH = 8
NKV = 2
QR = 1024
EPS = 1e-5
ISQ = 1.0 / math.sqrt(128.0)

ENGS = ("sync", "gpsimd", "scalar", "vector", "tensor")


class Prog:
    def __init__(self):
        self.ops = {e: [] for e in ENGS}
        self.n = {}

    def emit(self, eng, fn):
        self.ops[eng].append(fn)

    def wait(self, eng, sem, val):
        if val and val > 0:
            self.emit(eng, lambda E, s=sem, v=val: E.wait_ge(E._sems[s], v))

    def bump(self, sem, amt):
        self.n[sem] = self.n.get(sem, 0) + amt
        return self.n[sem]


def build_nc(dbg=False):
    nc = bass.Bass("TRN2", target_bir_lowering=False)

    xb = nc.dram_tensor("xb", [T, D], F32, kind="ExternalInput").ap()
    xq = nc.dram_tensor("xq", [QR, D], F32, kind="ExternalInput").ap()
    wqt = nc.dram_tensor("wqt", [D, NH * DH], F32, kind="ExternalInput").ap()
    wkt = nc.dram_tensor("wkt", [D, NKV * DH], F32, kind="ExternalInput").ap()
    wvt = nc.dram_tensor("wvt", [D, NKV * DH], F32, kind="ExternalInput").ap()
    wot = nc.dram_tensor("wot", [D, D], F32, kind="ExternalInput").ap()
    itt = nc.dram_tensor("itt", [D, D], F32, kind="ExternalInput").ap()
    cost = nc.dram_tensor("cost", [DH, T], F32, kind="ExternalInput").ap()
    sint = nc.dram_tensor("sint", [DH, T], F32, kind="ExternalInput").ap()
    maskt = nc.dram_tensor("maskt", [128, 896], BF16, kind="ExternalInput").ap()
    identt = nc.dram_tensor("identt", [128, 128], BF16, kind="ExternalInput").ap()
    p64t = nc.dram_tensor("p64t", [128, 128], BF16, kind="ExternalInput").ap()
    y = nc.dram_tensor("y", [QR, D], F32, kind="ExternalOutput").ap()

    lnT_sc = nc.dram_tensor("lnT_sc", [D, T], BF16, kind="Internal").ap()
    x2_sc = nc.dram_tensor("x2_sc", [QR, D], F32, kind="Internal").ap()
    cc_in = [nc.dram_tensor(f"cc_in{h}", [4 * DH, T], BF16, kind="Internal").ap()
             for h in range(NKV)]
    cc_out = [nc.dram_tensor(f"cc_out{h}", [16 * DH, T], BF16, kind="Internal").ap()
              for h in range(NKV)]

    if dbg:
        d_lnT = nc.dram_tensor("d_lnT", [D, T], BF16, kind="ExternalOutput").ap()
        d_kT = nc.dram_tensor("d_kT", [128, 8192], BF16, kind="ExternalOutput").ap()
        d_V = nc.dram_tensor("d_V", [128, 8192], BF16, kind="ExternalOutput").ap()
        d_qT = nc.dram_tensor("d_qT", [128, 32768], BF16, kind="ExternalOutput").ap()
        d_cc = [nc.dram_tensor(f"d_cc{h}", [16 * DH, T], BF16, kind="ExternalOutput").ap()
                for h in range(NKV)]
        d_x2 = nc.dram_tensor("d_x2", [QR, D], F32, kind="ExternalOutput").ap()
        d_ctxF = nc.dram_tensor("d_ctxF", [128, 32768], BF16, kind="ExternalOutput").ap()
        d_op = nc.dram_tensor("d_op", [128, 512], F32, kind="ExternalOutput").ap()
        d_wo0 = nc.dram_tensor("d_wo0", [128, 16384], BF16, kind="ExternalOutput").ap()
        d_acc = nc.dram_tensor("d_acc", [128, 16], F32, kind="ExternalOutput").ap()

    P = Prog()

    with ExitStack() as ctx:
        e = ctx.enter_context

        AR_A = e(nc.sbuf_tensor("AR_A", [128, 32768], BF16))
        AR_B = e(nc.sbuf_tensor("AR_B", [128, 32768], BF16))
        AR_C = e(nc.sbuf_tensor("AR_C", [128, 16384], BF16))
        kT = e(nc.sbuf_tensor("kT", [128, NKV * T], BF16))
        Vsb = e(nc.sbuf_tensor("Vsb", [128, 8192], BF16))
        cos_sb = e(nc.sbuf_tensor("cos_sb", [128, 512], F32))
        sin_sb = e(nc.sbuf_tensor("sin_sb", [128, 512], F32))
        mask_sb = e(nc.sbuf_tensor("mask_sb", [128, 896], BF16))
        id_sb = e(nc.sbuf_tensor("id_sb", [128, 128], BF16))
        p64_sb = e(nc.sbuf_tensor("p64_sb", [128, 128], BF16))
        ones_bf = e(nc.sbuf_tensor("ones_bf", [128, 1], BF16))
        ones_f = e(nc.sbuf_tensor("ones_f", [1, 128], F32))
        ropeA = e(nc.sbuf_tensor("ropeA", [128, 512], F32))
        ropeB = e(nc.sbuf_tensor("ropeB", [128, 512], F32))
        qrot = e(nc.sbuf_tensor("qrot", [128, 512], BF16))
        vtst = e(nc.sbuf_tensor("vtst", [128, 1024], BF16))
        stats = e(nc.sbuf_tensor("stats", [128, 64], F32))
        p5st = e(nc.sbuf_tensor("p5st", [128, 128], F32))

        pbA = e(nc.psum_tensor("pbA", [128, 512], F32))
        pbB = e(nc.psum_tensor("pbB", [128, 512], F32))
        pbC = e(nc.psum_tensor("pbC", [128, 512], F32))
        pbD = e(nc.psum_tensor("pbD", [128, 512], F32))
        pbE = e(nc.psum_tensor("pbE", [128, 512], F32))
        pbF = e(nc.psum_tensor("pbF", [128, 512], F32))
        pbG = e(nc.psum_tensor("pbG", [128, 1024], BF16))
        pbH = e(nc.psum_tensor("pbH", [128, 1024], BF16))

        lnT = [AR_A[:, 0:16384], AR_A[:, 16384:32768]]
        xbuf = [AR_B[:, 0:8192].bitcast(F32), AR_B[:, 8192:16384].bitcast(F32)]
        lnbf = [AR_B[:, 16384:20480], AR_B[:, 20480:24576]]
        junk = AR_B[:, 24576:28672]
        wk_bf = AR_C[:, 0:8192]
        wv_bf = AR_C[:, 8192:16384]
        wq_bf = AR_C[:, 0:16384]
        qT = AR_B
        EB = [cos_sb[:, :].bitcast(BF16)[:, 0:512],
              cos_sb[:, :].bitcast(BF16)[:, 512:1024],
              sin_sb[:, :].bitcast(BF16)[:, 0:512]]
        ctxst = [vtst[:, 0:512], vtst[:, 512:1024]]
        ctxF = AR_A
        wo_bf = [AR_B[:, 0:16384], AR_B[:, 16384:32768]]
        it_bf = wo_bf
        kT_f32 = kT[:, :].bitcast(F32)
        xe_t = [kT_f32[:, 0:512], kT_f32[:, 512:1024]]
        x2st = [Vsb[:, 4096:5120].bitcast(F32)[:, 0:512],
                Vsb[:, 5120:6144].bitcast(F32)[:, 0:512]]
        j5 = sin_sb[:, :].bitcast(BF16)[:, 512:1024]
        ln2T = AR_A
        x2row = [AR_C[:, 0:8192].bitcast(F32), AR_C[:, 8192:16384].bitcast(F32)]
        ln2bf = [Vsb[:, 0:4096], Vsb[:, 4096:8192]]
        xe2 = [kT_f32[:, 1024:1536], kT_f32[:, 1536:2048]]
        yst = [ropeA[:, :], ropeB[:, :]]

        sem_names = ["qS", "qO", "qG", "cT", "cV", "cS"]
        sems = {nm: e(nc.semaphore(nm)) for nm in sem_names}
        for nm in sem_names:
            P.n[nm] = 0

        def dma(eng, out, in_, q):
            v = P.bump(q, 16)
            P.emit(eng, lambda E, o=out, i=in_, s=q: E.dma_start(out=o, in_=i)
                   .then_inc(E._sems[s], 16))
            return v

        def mm(out, lhsT, rhs, start, stop, inc=False):
            if inc:
                v = P.bump("cT", 1)
                P.emit("tensor", lambda E, o=out, l=lhsT, r=rhs, a=start, b=stop:
                       E.matmul(o, l, r, start=a, stop=b, skip_group_check=True)
                       .then_inc(E._sems["cT"], 1))
                return v
            P.emit("tensor", lambda E, o=out, l=lhsT, r=rhs, a=start, b=stop:
                   E.matmul(o, l, r, start=a, stop=b, skip_group_check=True))
            return None

        def mm_group(psum_ap, lhs_of, rhs_of, n=32):
            for d in range(n):
                mm(psum_ap, lhs_of(d), rhs_of(d), d == 0, d == n - 1, inc=(d == n - 1))
            return P.n["cT"]

        def v_op(fn, inc=False):
            if inc:
                v = P.bump("cV", 1)
                P.emit("vector", lambda E, f=fn: f(E).then_inc(E._sems["cV"], 1))
                return v
            P.emit("vector", lambda E, f=fn: f(E))
            return None

        def s_op(fn, inc=False):
            if inc:
                v = P.bump("cS", 1)
                P.emit("scalar", lambda E, f=fn: f(E).then_inc(E._sems["cS"], 1))
                return v
            P.emit("scalar", lambda E, f=fn: f(E))
            return None

        def t_op(fn, inc=False):
            if inc:
                v = P.bump("cT", 1)
                P.emit("tensor", lambda E, f=fn: f(E).then_inc(E._sems["cT"], 1))
                return v
            P.emit("tensor", lambda E, f=fn: f(E))
            return None

        S1, S2, MV, TM, VV, YY, T2 = range(7)

        def sA(arr, b0, nb):
            return stats[:, arr * 8 + b0: arr * 8 + b0 + nb]

        def sA0(arr, b):
            return stats[:, arr * 8 + b: arr * 8 + b + 1]

        def ln_chain(nb, spacer_ap_a, spacer_ap_b):
            def sp():
                v_op(lambda E: E.tensor_add(spacer_ap_a, spacer_ap_b,
                                            spacer_ap_b))

            v_op(lambda E: E.tensor_scalar_mul(sA(MV, 0, nb), sA(S1, 0, nb),
                                               -1.0 / D))
            steps = [
                lambda E: E.tensor_mul(sA(TM, 0, nb), sA(MV, 0, nb),
                                       sA(MV, 0, nb)),
                lambda E: E.scalar_tensor_tensor(
                    sA(VV, 0, nb), sA(S2, 0, nb), 1.0 / D, sA(TM, 0, nb),
                    ALU.mult, ALU.subtract),
                lambda E: E.tensor_scalar_add(sA(VV, 0, nb), sA(VV, 0, nb), EPS),
                lambda E: E.tensor_scalar_add(sA(T2, 0, nb), sA(VV, 0, nb), 1.0),
                lambda E: E.reciprocal(sA(YY, 0, nb), sA(T2, 0, nb)),
                lambda E: E.tensor_scalar_mul(sA(YY, 0, nb), sA(YY, 0, nb), 2.0),
            ]
            for _ in range(5):
                steps += [
                    lambda E: E.tensor_mul(sA(TM, 0, nb), sA(YY, 0, nb),
                                           sA(YY, 0, nb)),
                    lambda E: E.tensor_mul(sA(TM, 0, nb), sA(TM, 0, nb),
                                           sA(VV, 0, nb)),
                    lambda E: E.tensor_scalar_mul(sA(TM, 0, nb), sA(TM, 0, nb),
                                                  -0.5),
                    lambda E: E.tensor_scalar_add(sA(TM, 0, nb), sA(TM, 0, nb),
                                                  1.5),
                    lambda E: E.tensor_mul(sA(YY, 0, nb), sA(YY, 0, nb),
                                           sA(TM, 0, nb)),
                ]
            for fn in steps:
                sp()
                v_op(fn)
            return v_op(lambda E: E.tensor_add(spacer_ap_a, spacer_ap_b,
                                               spacer_ap_b), inc=True)

        dma("sync", mask_sb[:], maskt, "qS")
        dma("sync", id_sb[:], identt, "qS")
        qS_pre = dma("sync", p64_sb[:], p64t, "qS")
        v_op(lambda E: E.memset(ones_bf[:], 1.0))
        v_op(lambda E: E.memset(ones_f[:], 1.0))
        v_op(lambda E: E.memset(stats[:], 1.0), inc=True)
        for a in range(4):
            dma("gpsimd",
                wk_bf.rearrange("p (a c) -> p a c", c=256)[:, 8 * a:8 * a + 8],
                wkt.rearrange("(a p) c -> p a c", p=128)[:, 8 * a:8 * a + 8], "qG")
        for a in range(4):
            dma("gpsimd",
                wv_bf.rearrange("p (a c) -> p a c", c=256)[:, 8 * a:8 * a + 8],
                wvt.rearrange("(a p) c -> p a c", p=128)[:, 8 * a:8 * a + 8], "qG")
        qG_wkv = P.n["qG"]

        snap_norm = {}
        snap_red = {}
        snap_tile_tr = {}
        snap_trgrp = {}
        snap_trev = {}
        snap_lnTmm = {}
        snap_spill = {}
        snap_lnTev = {}
        snap_rope = {}
        snap_vcp = {}
        snap_cos = {}
        snap_ropedone = {}

        gi_ctr = [0]
        ab_ctr = [0]
        TRB = [pbG, pbH]
        PAB = [pbA[:, :], pbB[:, :]]
        PEF = [pbE[:, :], pbF[:, :]]
        PCD = [pbC[:, :], pbD[:, :]]

        def transpose_group(src_of, dest_ap, n=8):
            gi = gi_ctr[0]
            bank = TRB[gi % 2]
            if gi - 2 in snap_trev:
                P.wait("tensor", "cV", snap_trev[gi - 2])
            for j in range(n):
                if j == n - 1:
                    snap_trgrp[gi] = P.bump("cT", 1)
                    P.emit("tensor", lambda E, o=bank[:, j * 128:(j + 1) * 128],
                           i=src_of(j): E.transpose(o, i, id_sb[:])
                           .then_inc(E._sems["cT"], 1))
                else:
                    P.emit("tensor", lambda E, o=bank[:, j * 128:(j + 1) * 128],
                           i=src_of(j): E.transpose(o, i, id_sb[:]))
            P.wait("vector", "cT", snap_trgrp[gi])
            src3 = bank[:, 0:n * 128].rearrange("p (a t) -> p a t", t=128)
            snap_trev[gi] = v_op(
                lambda E, d=dest_ap, s=src3: E.tensor_copy(out=d, in_=s), inc=True)
            gi_ctr[0] += 1
            return snap_trev[gi]

        def rope_emit(proj_psum, dest_ap):
            ab = ab_ctr[0]
            cp = v_op(lambda E, p=proj_psum: E.tensor_copy(out=qrot[:], in_=p),
                      inc=True)
            rb = PEF[ab % 2]
            P.wait("tensor", "cV", cp)
            if ab - 2 in snap_rope:
                P.wait("tensor", "cV", snap_rope[ab - 2])
            rt = t_op(lambda E, r=rb: E.matmul(
                r, p64_sb[:], qrot[:], start=True, stop=True,
                skip_group_check=True), inc=True)
            P.wait("vector", "cT", rt)
            v_op(lambda E, p=proj_psum: E.tensor_mul(ropeA[:], p, cos_sb[:]))
            v_op(lambda E, r=rb: E.tensor_mul(ropeB[:], r, sin_sb[:]))
            snap_rope[ab] = v_op(
                lambda E, d=dest_ap: E.tensor_add(d, ropeA[:], ropeB[:]), inc=True)
            ab_ctr[0] += 1
            return snap_rope[ab]

        for c in range(8):
            if c - 1 in snap_ropedone:
                P.wait("sync", "cV", snap_ropedone[c - 1])
            dma("sync", cos_sb[:], cost[:, 512 * c:512 * (c + 1)], "qS")
            snap_cos[c] = dma("sync", sin_sb[:], sint[:, 512 * c:512 * (c + 1)], "qS")

            for st in range(4):
                ti = 4 * c + st
                if st >= 2:
                    P.wait("sync", "cV", snap_red[ti - 2])
                elif c >= 1:
                    P.wait("sync", "cV", snap_norm[4 * c - 2 + st])
                qs_x = dma("sync", xbuf[st % 2][:, :],
                           xb[128 * ti:128 * (ti + 1), :], "qS")
                P.wait("vector", "qS", qs_x)
                xt = xbuf[st % 2][:, :]
                v_op(lambda E, x=xt, o=sA0(S1, st): E.reduce_sum(o, x, axis=AX.X))
                snap_red[ti] = v_op(
                    lambda E, x=xt, o=sA0(S2, st): E.scalar_tensor_tensor(
                        junk[:], x, 1.0, x, ALU.mult, ALU.mult, accum_out=o),
                    inc=True)
            jt = junk.rearrange("p (a t) -> p a t", t=1024)
            ln_chain(4, jt[:, 2, :], jt[:, 3, :])
            for st in range(4):
                ti = 4 * c + st
                if st >= 2:
                    P.wait("sync", "cV", snap_norm[ti - 2])
                else:
                    P.wait("sync", "cV", snap_red[4 * c + 2 + st])
                qs_x = dma("sync", xbuf[st % 2][:, :],
                           xb[128 * ti:128 * (ti + 1), :], "qS")
                P.wait("vector", "qS", qs_x)
                xt = xbuf[st % 2][:, :]
                if ti - 2 in snap_tile_tr:
                    P.wait("vector", "cT", snap_tile_tr[ti - 2])
                snap_norm[ti] = v_op(
                    lambda E, x=xt, o=lnbf[st % 2][:, :], s_=st:
                    E.tensor_scalar(o, x, sA0(MV, s_), sA0(YY, s_),
                                    ALU.add, ALU.mult), inc=True)
                P.wait("tensor", "cV", snap_norm[ti])
                if ti == 0:
                    P.wait("tensor", "qS", qS_pre)
                for grp in range(4):
                    if st == 0 and grp == 0:
                        if c - 2 in snap_lnTmm:
                            P.wait("vector", "cT", snap_lnTmm[c - 2])
                            P.wait("vector", "qO", snap_spill[c - 2])
                    src = lnbf[st % 2]
                    dst = lnT[c % 2].rearrange("p (a t) -> p a t", t=512)[
                        :, 8 * grp:8 * grp + 8, st * 128:(st + 1) * 128]
                    transpose_group(
                        lambda j, s=src, g=grp:
                        s[:, (8 * g + j) * 128:(8 * g + j + 1) * 128], dst)
                snap_tile_tr[ti] = P.n["cT"]
            snap_lnTev[c] = P.n["cV"]

            lnc = lnT[c % 2].rearrange("p (a t) -> p a t", t=512)
            if c == 0:
                P.wait("tensor", "qG", qG_wkv)
            for kk in range(NKV):
                bank = PAB[ab_ctr[0] % 2]
                if ab_ctr[0] - 2 in snap_rope:
                    P.wait("tensor", "cV", snap_rope[ab_ctr[0] - 2])
                P.wait("tensor", "cV", snap_lnTev[c])
                kmm = mm_group(
                    bank,
                    lambda d, k=kk: wk_bf[:, d * 256 + k * 128: d * 256 + (k + 1) * 128],
                    lambda d: lnc[:, d, :])
                P.wait("vector", "cT", kmm)
                P.wait("vector", "qS", snap_cos[c])
                rope_emit(bank, kT[:, kk * T + 512 * c: kk * T + 512 * (c + 1)])
            snap_ropedone[c] = P.n["cV"]
            for vv in range(NKV):
                bank = PCD[vv]
                if (c - 1, vv) in snap_vcp:
                    P.wait("tensor", "cV", snap_vcp[(c - 1, vv)])
                vmm = mm_group(
                    bank,
                    lambda d, k=vv: wv_bf[:, d * 256 + k * 128: d * 256 + (k + 1) * 128],
                    lambda d: lnc[:, d, :])
                P.wait("vector", "cT", vmm)
                snap_vcp[(c, vv)] = v_op(
                    lambda E, b=bank, o=vtst[:, vv * 512:(vv + 1) * 512]:
                    E.tensor_copy(out=o, in_=b), inc=True)
            P.wait("tensor", "cV", snap_vcp[(c, 1)])
            transpose_group(
                lambda j: vtst[:, (j % 2) * 512 + (j // 2) * 128:
                               (j % 2) * 512 + (j // 2) * 128 + 128],
                Vsb[:, c * 1024:(c + 1) * 1024].rearrange(
                    "p (a t) -> p a t", t=128))
            snap_lnTmm[c] = P.n["cT"]
            P.wait("scalar", "cV", snap_lnTev[c])
            for a in range(4):
                dma("scalar",
                    lnT_sc.rearrange("(a p) t -> p a t", p=128)[
                        :, 8 * a:8 * a + 8, 512 * c:512 * (c + 1)],
                    lnT[c % 2].rearrange("p (a t) -> p a t", t=512)[
                        :, 8 * a:8 * a + 8, :], "qO")
            snap_spill[c] = P.n["qO"]

        cT_P1 = P.n["cT"]
        cV_P1 = P.n["cV"]
        cS_P1 = P.n["cS"]

        snap_p2mm = {}
        for qp in range(2):
            if qp == 0:
                P.wait("gpsimd", "cT", cT_P1)
            else:
                P.wait("gpsimd", "cT", snap_p2mm[(0, 7)])
            for a in range(4):
                dma("gpsimd",
                    wq_bf.rearrange("p (a c) -> p a c", c=512)[:, 8 * a:8 * a + 8],
                    wqt.rearrange("(a p) c -> p a c", p=128)[
                        :, 8 * a:8 * a + 8, 512 * qp:512 * (qp + 1)], "qG")
            qG_wq = P.n["qG"]
            for c in range(8):
                if qp == 0 and c < 2:
                    P.wait("sync", "cT", cT_P1)
                    P.wait("sync", "qO", snap_spill[7])
                prev = (qp, c - 2) if c >= 2 else ((qp - 1, 6 + c) if qp else None)
                if prev in snap_p2mm:
                    P.wait("sync", "cT", snap_p2mm[prev])
                for a in range(4):
                    dma("sync",
                        lnT[c % 2].rearrange("p (a t) -> p a t", t=512)[
                            :, 8 * a:8 * a + 8, :],
                        lnT_sc.rearrange("(a p) t -> p a t", p=128)[
                            :, 8 * a:8 * a + 8, 512 * c:512 * (c + 1)], "qS")
                qs_ln = P.n["qS"]
                key_prev = ("p2", qp, c - 1) if c else (
                    ("p2", 0, 7) if qp else 7)
                if key_prev in snap_ropedone:
                    P.wait("sync", "cV", snap_ropedone[key_prev])
                dma("sync", cos_sb[:], cost[:, 512 * c:512 * (c + 1)], "qS")
                sc_cos = dma("sync", sin_sb[:], sint[:, 512 * c:512 * (c + 1)], "qS")
                lnc = lnT[c % 2].rearrange("p (a t) -> p a t", t=512)
                for hh in range(4):
                    H = 4 * qp + hh
                    bank = PAB[ab_ctr[0] % 2]
                    if ab_ctr[0] - 2 in snap_rope:
                        P.wait("tensor", "cV", snap_rope[ab_ctr[0] - 2])
                    P.wait("tensor", "qS", qs_ln)
                    if hh == 0 and c == 0:
                        P.wait("tensor", "qG", qG_wq)
                    qmm = mm_group(
                        bank,
                        lambda d, h=hh: wq_bf[:, d * 512 + h * 128:
                                              d * 512 + (h + 1) * 128],
                        lambda d: lnc[:, d, :])
                    P.wait("vector", "cT", qmm)
                    P.wait("vector", "qS", sc_cos)
                    if qp == 0 and c == 0 and hh == 0:
                        P.wait("vector", "cT", cT_P1)
                    rope_emit(bank, qT[:, H * T + 512 * c: H * T + 512 * (c + 1)])
                snap_ropedone[("p2", qp, c)] = P.n["cV"]
                snap_p2mm[(qp, c)] = P.n["cT"]

        cT_P2 = P.n["cT"]
        cV_P2 = P.n["cV"]

        if dbg:
            P.wait("sync", "cV", cV_P2)
            dma("sync", d_kT, kT[:, :], "qS")
            dma("sync", d_V, Vsb[:, :], "qS")
            dma("sync", d_qT, qT[:, :], "qS")

        snap_exp = {}
        snap_mask = {}
        snap_ctxmm = {}
        snap_iter = {}
        snap_ctxdma = {}
        kglob = 0
        it_ctr = 0
        h_end_cT = {}
        h_ctx_qO = {}
        RB = pbG[:, :].bitcast(F32)

        for h in range(NKV):
            for g in range(4):
                H = 4 * h + g
                for s in range(8):
                    nk = 4 * s + 4
                    su = PEF[it_ctr % 2]
                    cx = PCD[it_ctr % 2]
                    if it_ctr - 2 in snap_iter:
                        P.wait("tensor", "cV", snap_iter[it_ctr - 2])
                    for k in range(nk):
                        sb_ = PAB[kglob % 2]
                        eb = EB[kglob % 3]
                        if kglob - 2 in snap_exp:
                            P.wait("tensor", "cS", snap_exp[kglob - 2])
                        if kglob == 0:
                            P.wait("tensor", "cV", cV_P2)
                        smm = mm(sb_,
                                 kT[:, h * T + 128 * k: h * T + 128 * (k + 1)],
                                 qT[:, H * T + 512 * s: H * T + 512 * (s + 1)],
                                 True, True, inc=True)
                        if kglob - 3 in snap_ctxmm:
                            P.wait("scalar", "cT", snap_ctxmm[kglob - 3])
                        P.wait("scalar", "cT", smm)
                        if kglob == 0:
                            P.wait("scalar", "cV", cV_P2)
                        snap_exp[kglob] = s_op(
                            lambda E, p=sb_, o=eb: E.activation(
                                o, p, AF.Exp, scale=ISQ), inc=True)
                        if k >= 4 * s:
                            off = 384 - (128 * k - 512 * s)
                            P.wait("vector", "cS", snap_exp[kglob])
                            snap_mask[kglob] = v_op(
                                lambda E, o=eb, m=mask_sb[:, off:off + 512]:
                                E.tensor_tensor(out=o, in0=o, in1=m, op=ALU.mult),
                                inc=True)
                            P.wait("tensor", "cV", snap_mask[kglob])
                        else:
                            P.wait("tensor", "cS", snap_exp[kglob])
                        mm(su[0:1, :], ones_bf[:], eb, k == 0, k == nk - 1)
                        snap_ctxmm[kglob] = mm(
                            cx, Vsb[:, k * 256 + h * 128: k * 256 + (h + 1) * 128],
                            eb, k == 0, k == nk - 1, inc=True)
                        kglob += 1
                    P.wait("vector", "cT", snap_ctxmm[kglob - 1])
                    rec = ropeB[0:1, :]
                    vrec = v_op(lambda E, s_=su: E.reciprocal(rec, s_[0:1, :]),
                                inc=True)
                    P.wait("tensor", "cV", vrec)
                    romm = t_op(lambda E: E.matmul(
                        RB, ones_f[:], rec, start=True, stop=True,
                        skip_group_check=True), inc=True)
                    P.wait("vector", "cT", romm)
                    v_op(lambda E: E.tensor_copy(out=ropeA[:], in_=RB))
                    if it_ctr - 2 in snap_ctxdma:
                        P.wait("vector", "qO", snap_ctxdma[it_ctr - 2])
                    cst = ctxst[it_ctr % 2]
                    vdone = v_op(lambda E, c_=cx, o=cst: E.tensor_tensor(
                        out=o, in0=c_, in1=ropeA[:], op=ALU.mult), inc=True)
                    snap_iter[it_ctr] = vdone
                    P.wait("scalar", "cV", vdone)
                    snap_ctxdma[it_ctr] = dma(
                        "scalar",
                        cc_in[h][128 * g:128 * (g + 1), 512 * s:512 * (s + 1)],
                        cst, "qO")
                    it_ctr += 1
            h_end_cT[h] = P.n["cT"]
            h_ctx_qO[h] = P.n["qO"]

        def emit_ag(h):
            P.wait("gpsimd", "qO", h_ctx_qO[h])
            v = P.bump("qG", 1)
            P.emit("gpsimd", lambda E, hh=h: E.collective_compute(
                "AllGather", ALU.bypass,
                replica_groups=[[0, 1, 2, 3], [4, 5, 6, 7]],
                ins=[cc_in[hh]], outs=[cc_out[hh]]).then_inc(E._sems["qG"], 1))
            return v

        def emit_switch_loads(h):
            for _ in range(16):
                P.bump("qG", 16)

            def do_switch(E, hh=h):
                pid = E.partition_id()
                qv = E.compute_val(pid % 4)
                for body in nc.Switch(engines=[E], index=[qv], n=4):
                    for blk in range(16):
                        E.dma_start(
                            out=ctxF[:, (hh * 16 + blk) * 1024:
                                     (hh * 16 + blk + 1) * 1024],
                            in_=cc_out[hh][blk * 128:(blk + 1) * 128,
                                           body * 1024:(body + 1) * 1024],
                        ).then_inc(E._sems["qG"], 16)
            P.emit("gpsimd", do_switch)
            return P.n["qG"]

        def emit_wo_cast(e_, wait_cT):
            P.wait("gpsimd", "cT", wait_cT)
            for a in range(4):
                dma("gpsimd",
                    wo_bf[e_ % 2].rearrange("p (a c) -> p a c", c=512)[
                        :, 8 * a:8 * a + 8],
                    wot.rearrange("(a p) c -> p a c", p=128)[
                        :, 8 * a:8 * a + 8, 512 * e_:512 * (e_ + 1)], "qG")
            return P.n["qG"]

        ag0 = emit_ag(0)
        P.wait("gpsimd", "cT", cT_P2)
        P.wait("gpsimd", "qG", ag0)
        qG_ld0 = emit_switch_loads(0)
        snap_wocast = {0: emit_wo_cast(0, h_end_cT[0])}
        ag1 = emit_ag(1)
        P.wait("gpsimd", "qG", ag1)
        qG_ld1 = emit_switch_loads(1)
        snap_wocast[1] = emit_wo_cast(1, h_end_cT[1])

        snap_p5mm = {}
        snap_evac5 = {}
        snap_x2dma = {}
        ctxFv = ctxF.rearrange("p (a t) -> p a t", t=1024)

        if dbg:
            P.wait("sync", "qG", qG_ld1)
            dma("sync", d_ctxF, ctxF[:, :], "qS")
            P.wait("sync", "qG", snap_wocast[0])
            dma("sync", d_wo0, wo_bf[0], "qS")

        xeC = kT_f32[:, :].rearrange("p (a t) -> p a t", t=512)
        snap_xeC = {}
        tt_ctr = 0
        for e_ in range(8):
            wob = wo_bf[e_ % 2].rearrange("p (a c) -> p a c", c=512)
            if e_ == 0:
                P.wait("sync", "cT", h_end_cT[1])
            else:
                P.wait("sync", "cV", snap_evac5[8 * e_ - 1])
            snap_xeC[e_] = dma(
                "sync", xeC,
                xq.rearrange("(a p) d -> p a d", p=128)[:, :,
                                                        512 * e_:512 * (e_ + 1)],
                "qS")
            for tt in range(8):
                bank = PAB[tt_ctr % 2]
                if tt_ctr - 2 in snap_evac5:
                    P.wait("tensor", "cV", snap_evac5[tt_ctr - 2])
                if tt == 0:
                    P.wait("tensor", "qG", snap_wocast[e_])
                    if e_ == 0:
                        P.wait("tensor", "qG", qG_ld1)
                xet = xeC[:, tt, :]
                qs_xe = snap_xeC[e_]
                pm = mm_group(bank,
                              lambda d, t_=tt: ctxFv[:, d, t_ * 128:(t_ + 1) * 128],
                              lambda d: wob[:, d, :])
                snap_p5mm[(e_, tt)] = pm
                if dbg and e_ == 0 and tt == 0:
                    P.wait("vector", "cT", pm)
                    vm5 = v_op(lambda E, p=bank: E.tensor_copy(
                        out=ropeA[:], in_=p), inc=True)
                    P.wait("scalar", "cV", vm5)
                    qp5 = dma("scalar", d_op, ropeA[:], "qO")
                    P.wait("vector", "qO", qp5)
                P.wait("vector", "cT", pm)
                P.wait("vector", "qS", qs_xe)
                if tt_ctr - 2 in snap_x2dma:
                    P.wait("vector", "qO", snap_x2dma[tt_ctr - 2])
                x2s = x2st[tt_ctr % 2]
                c1 = p5st[:, tt * 16 + e_: tt * 16 + e_ + 1]
                c2 = p5st[:, tt * 16 + 8 + e_: tt * 16 + 8 + e_ + 1]
                v_op(lambda E, b=bank, x=xet, o=x2s, a=c1: E.scalar_tensor_tensor(
                    o, b, 1.0, x, ALU.mult, ALU.add, accum_out=a))
                j5 = Vsb[:, 2048:2560]
                snap_evac5[tt_ctr] = v_op(
                    lambda E, o=j5, x=x2s, a=c2: E.scalar_tensor_tensor(
                        o, x, 1.0, x, ALU.mult, ALU.mult, accum_out=a), inc=True)
                P.wait("scalar", "cV", snap_evac5[tt_ctr])
                snap_x2dma[tt_ctr] = dma(
                    "scalar",
                    x2_sc[128 * tt:128 * (tt + 1), 512 * e_:512 * (e_ + 1)],
                    x2s, "qO")
                tt_ctr += 1
            if e_ + 2 <= 7:
                snap_wocast[e_ + 2] = emit_wo_cast(e_ + 2, snap_p5mm[(e_, 7)])

        cV_P5 = P.n["cV"]
        qO_P5 = P.n["qO"]

        snap_n2 = {}
        snap_tr2t = {}
        for tt in range(8):
            v_op(lambda E, o=sA0(S1, tt), i=p5st[:, tt * 16:tt * 16 + 8]:
                 E.reduce_sum(o, i, axis=AX.X))
            v_op(lambda E, o=sA0(S2, tt), i=p5st[:, tt * 16 + 8:tt * 16 + 16]:
                 E.reduce_sum(o, i, axis=AX.X))
        ln_chain(8, kT_f32[:, 2048:2560], kT_f32[:, 2560:3072])
        for tt in range(8):
            par = tt % 2
            if tt - 2 in snap_n2:
                P.wait("sync", "cV", snap_n2[tt - 2])
            if tt == 0:
                P.wait("sync", "qO", qO_P5)
            qs_x2 = dma("sync", x2row[par][:, :],
                        x2_sc[128 * tt:128 * (tt + 1), :], "qS")
            P.wait("vector", "qS", qs_x2)
            if tt - 2 in snap_tr2t:
                P.wait("vector", "cT", snap_tr2t[tt - 2])
            snap_n2[tt] = v_op(
                lambda E, x=x2row[par][:, :], o=ln2bf[par][:, :], t_=tt:
                E.tensor_scalar(o, x, sA0(MV, t_), sA0(YY, t_),
                                ALU.add, ALU.mult), inc=True)
            P.wait("tensor", "cV", snap_n2[tt])
            if tt == 0:
                P.wait("tensor", "cV", cV_P5)
            for grp in range(4):
                src = ln2bf[par]
                dst = ln2T.rearrange("p (a t) -> p a t", t=1024)[
                    :, 8 * grp:8 * grp + 8, tt * 128:(tt + 1) * 128]
                transpose_group(
                    lambda j, s=src, g=grp:
                    s[:, (8 * g + j) * 128:(8 * g + j + 1) * 128], dst)
            snap_tr2t[tt] = P.n["cT"]
        cV_ln2 = P.n["cV"]

        snap_itcast = {}
        snap_f6mm = {}
        snap_yev = {}
        snap_ydma = {}

        def emit_it_cast(e_, wait_cT):
            P.wait("gpsimd", "cT", wait_cT)
            for a in range(4):
                dma("gpsimd",
                    it_bf[e_ % 2].rearrange("p (a c) -> p a c", c=512)[
                        :, 8 * a:8 * a + 8],
                    itt.rearrange("(a p) c -> p a c", p=128)[
                        :, 8 * a:8 * a + 8, 512 * e_:512 * (e_ + 1)], "qG")
            return P.n["qG"]

        snap_itcast[0] = emit_it_cast(0, snap_p5mm[(6, 7)])
        snap_itcast[1] = emit_it_cast(1, snap_p5mm[(7, 7)])

        ln2Tv = ln2T.rearrange("p (a t) -> p a t", t=1024)
        tt6 = 0
        for e_ in range(8):
            itb = it_bf[e_ % 2].rearrange("p (a c) -> p a c", c=512)
            for tt in range(8):
                bank = PAB[tt6 % 2]
                if tt6 - 2 in snap_yev:
                    P.wait("tensor", "cV", snap_yev[tt6 - 2])
                if tt == 0:
                    P.wait("tensor", "qG", snap_itcast[e_])
                    if e_ == 0:
                        P.wait("tensor", "cV", cV_ln2)
                xet2 = xe2[tt6 % 2]
                if tt6 - 2 in snap_yev:
                    P.wait("sync", "cV", snap_yev[tt6 - 2])
                qs_xe2 = dma(
                    "sync", xet2,
                    x2_sc[128 * tt:128 * (tt + 1), 512 * e_:512 * (e_ + 1)], "qS")
                fm = mm_group(bank,
                              lambda d, t_=tt: ln2Tv[:, d, t_ * 128:(t_ + 1) * 128],
                              lambda d: itb[:, d, :])
                snap_f6mm[(e_, tt)] = fm
                P.wait("vector", "cT", fm)
                P.wait("vector", "qS", qs_xe2)
                if tt6 - 2 in snap_ydma:
                    P.wait("vector", "qO", snap_ydma[tt6 - 2])
                yb = yst[tt6 % 2]
                snap_yev[tt6] = v_op(
                    lambda E, b=bank, x=xet2, o=yb: E.tensor_add(o, b, x), inc=True)
                P.wait("scalar", "cV", snap_yev[tt6])
                snap_ydma[tt6] = dma(
                    "scalar", y[128 * tt:128 * (tt + 1), 512 * e_:512 * (e_ + 1)],
                    yb, "qO")
                tt6 += 1
            if e_ + 2 <= 7:
                snap_itcast[e_ + 2] = emit_it_cast(e_ + 2, snap_f6mm[(e_, 7)])

        if dbg:
            P.wait("sync", "qG", P.n["qG"])
            P.wait("sync", "qO", P.n["qO"])
            for h in range(NKV):
                dma("sync", d_cc[h], cc_out[h], "qS")
            dma("sync", d_lnT, lnT_sc, "qS")
            dma("sync", d_x2, x2_sc, "qS")
            dma("sync", d_acc[:, 0:16], p5st[:, 0:16], "qS")

        P.wait("sync", "qO", P.n["qO"])
        P.wait("sync", "qG", P.n["qG"])
        P.wait("sync", "qS", P.n["qS"])

        with nc.Block() as block:

            def run(E, eng):
                E._sems = sems
                for fn in P.ops[eng]:
                    fn(E)

            @block.sync
            def _(E):
                run(E, "sync")

            @block.gpsimd
            def _(E):
                run(E, "gpsimd")

            @block.scalar
            def _(E):
                run(E, "scalar")

            @block.vector
            def _(E):
                run(E, "vector")

            @block.tensor
            def _(E):
                run(E, "tensor")

    return nc


def _rope_tables():
    ROPE_THETA = 500000.0
    ROPE_FACTOR = 32.0
    HI, LO = 4.0, 1.0
    ORIG = 8192
    d = np.arange(0, DH, 2, dtype=np.float64) / DH
    inv = 1.0 / ROPE_THETA ** d
    wav = 2.0 * math.pi / inv
    cut_lo = ORIG / HI
    cut_hi = ORIG / LO
    inv = np.where(wav > cut_hi, inv / ROPE_FACTOR, inv)
    smooth = (ORIG / wav - LO) / (HI - LO)
    smoothed = (1.0 - smooth) * inv / ROPE_FACTOR + smooth * inv
    med = (~(wav < cut_hi)) & (~(wav > cut_lo))
    inv = np.where(med, smoothed, inv)
    pos = np.arange(T, dtype=np.float64)
    freqs = pos[:, None] * inv[None, :]
    emb = np.concatenate([freqs, freqs], axis=1)
    cos = np.ascontiguousarray(np.cos(emb).T.astype(np.float32))
    sin = np.sin(emb).T.astype(np.float32)
    sin_signed = sin.copy()
    sin_signed[0:64] = -sin[0:64]
    return cos, np.ascontiguousarray(sin_signed)


def kernel(L0_attn_x, L0_attn_Wq, L0_attn_Wk, L0_attn_Wv, L0_attn_Wo,
           L0_ffn_identity):
    x = np.asarray(L0_attn_x, dtype=np.float32)
    Wq = np.asarray(L0_attn_Wq, dtype=np.float32)
    Wk = np.asarray(L0_attn_Wk, dtype=np.float32)
    Wv = np.asarray(L0_attn_Wv, dtype=np.float32)
    Wo = np.asarray(L0_attn_Wo, dtype=np.float32)
    I = np.asarray(L0_ffn_identity, dtype=np.float32)

    cosT, sinT = _rope_tables()
    mask = np.zeros((128, 896), dtype=ml_dtypes.bfloat16)
    for p in range(128):
        mask[p, p + 384:] = 1.0
    ident = np.eye(128, dtype=ml_dtypes.bfloat16)
    p64 = np.zeros((128, 128), dtype=ml_dtypes.bfloat16)
    for i in range(128):
        p64[i, i ^ 64] = 1.0

    perm = []
    for h in range(2):
        for r in range(4):
            for g in range(4):
                head = 8 * r + 4 * h + g
                perm.extend(range(head * 128, head * 128 + 128))
    WoT_perm = np.ascontiguousarray(Wo.T[np.array(perm)])
    IT = np.ascontiguousarray(I.T)

    in_maps = []
    for c in range(8):
        b, qq = c // 4, c % 4
        in_maps.append({
            "xb": np.ascontiguousarray(x[b]),
            "xq": np.ascontiguousarray(x[b, 1024 * qq:1024 * (qq + 1)]),
            "wqt": np.ascontiguousarray(Wq[1024 * qq:1024 * (qq + 1), :].T),
            "wkt": np.ascontiguousarray(Wk[256 * qq:256 * (qq + 1), :].T),
            "wvt": np.ascontiguousarray(Wv[256 * qq:256 * (qq + 1), :].T),
            "wot": WoT_perm,
            "itt": IT,
            "cost": cosT,
            "sint": sinT,
            "maskt": mask,
            "identt": ident,
            "p64t": p64,
        })

    nc = build_nc()
    res = run_bass_kernel_spmd(nc, in_maps, core_ids=list(range(8)))
    out = np.empty((2, T, D), dtype=np.float32)
    for c in range(8):
        b, qq = c // 4, c % 4
        out[b, 1024 * qq:1024 * (qq + 1)] = res.results[c]["y"]
    return out


# version 03
# speedup vs baseline: 1.0456x; 1.0456x over previous
import math
from contextlib import ExitStack

import numpy as np
import ml_dtypes

import concourse.bass as bass
import concourse.mybir as mybir
from concourse.bass_utils import run_bass_kernel_spmd

F32 = mybir.dt.float32
BF16 = mybir.dt.bfloat16
AF = mybir.ActivationFunctionType
ALU = mybir.AluOpType
AX = mybir.AxisListType

T = 4096
D = 4096
DH = 128
NH = 8
NKV = 2
QR = 1024
EPS = 1e-5
ISQ = 1.0 / math.sqrt(128.0)

ENGS = ("sync", "gpsimd", "scalar", "vector", "tensor")


class Prog:
    def __init__(self):
        self.ops = {e: [] for e in ENGS}
        self.n = {}

    def emit(self, eng, fn):
        self.ops[eng].append(fn)

    def wait(self, eng, sem, val):
        if val and val > 0:
            self.emit(eng, lambda E, s=sem, v=val: E.wait_ge(E._sems[s], v))

    def bump(self, sem, amt):
        self.n[sem] = self.n.get(sem, 0) + amt
        return self.n[sem]


def build_nc(dbg=False):
    nc = bass.Bass("TRN2", target_bir_lowering=False)

    xb = nc.dram_tensor("xb", [T, D], F32, kind="ExternalInput").ap()
    xq = nc.dram_tensor("xq", [QR, D], F32, kind="ExternalInput").ap()
    wqt = nc.dram_tensor("wqt", [D, NH * DH], F32, kind="ExternalInput").ap()
    wkt = nc.dram_tensor("wkt", [D, NKV * DH], F32, kind="ExternalInput").ap()
    wvt = nc.dram_tensor("wvt", [D, NKV * DH], F32, kind="ExternalInput").ap()
    wot = nc.dram_tensor("wot", [D, D], F32, kind="ExternalInput").ap()
    itt = nc.dram_tensor("itt", [D, D], F32, kind="ExternalInput").ap()
    cost = nc.dram_tensor("cost", [DH, T], F32, kind="ExternalInput").ap()
    sint = nc.dram_tensor("sint", [DH, T], F32, kind="ExternalInput").ap()
    maskt = nc.dram_tensor("maskt", [128, 896], BF16, kind="ExternalInput").ap()
    identt = nc.dram_tensor("identt", [128, 128], BF16, kind="ExternalInput").ap()
    p64t = nc.dram_tensor("p64t", [128, 128], BF16, kind="ExternalInput").ap()
    y = nc.dram_tensor("y", [QR, D], F32, kind="ExternalOutput").ap()

    lnT_sc = nc.dram_tensor("lnT_sc", [D, T], BF16, kind="Internal").ap()
    x2_sc = nc.dram_tensor("x2_sc", [QR, D], F32, kind="Internal").ap()
    cc_in = [[nc.dram_tensor(f"cc_in{h}_{qt}", [4 * DH, 1024], BF16,
                             kind="Internal").ap() for qt in range(4)]
             for h in range(NKV)]
    cc_out = [[nc.dram_tensor(f"cc_out{h}_{qt}", [16 * DH, 1024], BF16,
                              kind="Internal").ap() for qt in range(4)]
              for h in range(NKV)]

    if dbg:
        d_lnT = nc.dram_tensor("d_lnT", [D, T], BF16, kind="ExternalOutput").ap()
        d_kT = nc.dram_tensor("d_kT", [128, 8192], BF16, kind="ExternalOutput").ap()
        d_V = nc.dram_tensor("d_V", [128, 8192], BF16, kind="ExternalOutput").ap()
        d_qT = nc.dram_tensor("d_qT", [128, 32768], BF16, kind="ExternalOutput").ap()
        d_cc = [nc.dram_tensor(f"d_cc{h}", [16 * DH, T], BF16, kind="ExternalOutput").ap()
                for h in range(NKV)]
        d_x2 = nc.dram_tensor("d_x2", [QR, D], F32, kind="ExternalOutput").ap()
        d_ctxF = nc.dram_tensor("d_ctxF", [128, 32768], BF16, kind="ExternalOutput").ap()
        d_op = nc.dram_tensor("d_op", [128, 512], F32, kind="ExternalOutput").ap()
        d_wo0 = nc.dram_tensor("d_wo0", [128, 16384], BF16, kind="ExternalOutput").ap()
        d_acc = nc.dram_tensor("d_acc", [128, 16], F32, kind="ExternalOutput").ap()

    P = Prog()

    with ExitStack() as ctx:
        e = ctx.enter_context

        AR_A = e(nc.sbuf_tensor("AR_A", [128, 32768], BF16))
        AR_B = e(nc.sbuf_tensor("AR_B", [128, 32768], BF16))
        AR_C = e(nc.sbuf_tensor("AR_C", [128, 16384], BF16))
        kT = e(nc.sbuf_tensor("kT", [128, NKV * T], BF16))
        Vsb = e(nc.sbuf_tensor("Vsb", [128, 8192], BF16))
        cos_sb = e(nc.sbuf_tensor("cos_sb", [128, 512], F32))
        sin_sb = e(nc.sbuf_tensor("sin_sb", [128, 512], F32))
        mask_sb = e(nc.sbuf_tensor("mask_sb", [128, 896], BF16))
        id_sb = e(nc.sbuf_tensor("id_sb", [128, 128], BF16))
        p64_sb = e(nc.sbuf_tensor("p64_sb", [128, 128], BF16))
        ones_bf = e(nc.sbuf_tensor("ones_bf", [128, 128], BF16))
        ones_f = e(nc.sbuf_tensor("ones_f", [1, 128], F32))
        ropeA = e(nc.sbuf_tensor("ropeA", [128, 512], F32))
        ropeB = e(nc.sbuf_tensor("ropeB", [128, 512], F32))
        qrot = e(nc.sbuf_tensor("qrot", [128, 512], BF16))
        vtst = e(nc.sbuf_tensor("vtst", [128, 1024], BF16))
        stats = e(nc.sbuf_tensor("stats", [128, 64], F32))
        p5st = e(nc.sbuf_tensor("p5st", [128, 128], F32))

        pbA = e(nc.psum_tensor("pbA", [128, 512], F32))
        pbB = e(nc.psum_tensor("pbB", [128, 512], F32))
        pbC = e(nc.psum_tensor("pbC", [128, 512], F32))
        pbD = e(nc.psum_tensor("pbD", [128, 512], F32))
        pbE = e(nc.psum_tensor("pbE", [128, 512], F32))
        pbF = e(nc.psum_tensor("pbF", [128, 512], F32))
        pbG = e(nc.psum_tensor("pbG", [128, 1024], BF16))
        pbH = e(nc.psum_tensor("pbH", [128, 1024], BF16))

        lnT = [AR_A[:, 0:16384], AR_A[:, 16384:32768]]
        xbuf = [AR_B[:, 0:8192].bitcast(F32), AR_B[:, 8192:16384].bitcast(F32)]
        lnbf = [AR_B[:, 16384:20480], AR_B[:, 20480:24576]]
        junk = AR_B[:, 24576:28672]
        wk_bf = AR_C[:, 0:8192]
        wv_bf = AR_C[:, 8192:16384]
        wq_bf = AR_C[:, 0:16384]
        qT = AR_B
        EB = [cos_sb[:, :].bitcast(BF16)[:, 0:512],
              cos_sb[:, :].bitcast(BF16)[:, 512:1024],
              sin_sb[:, :].bitcast(BF16)[:, 0:512]]
        ctxst = [vtst[:, 0:512], vtst[:, 512:1024]]
        ctxF = AR_A
        wo_bf = [AR_B[:, 0:16384], AR_B[:, 16384:32768]]
        it_bf = wo_bf
        kT_f32 = kT[:, :].bitcast(F32)
        xe_t = [kT_f32[:, 0:512], kT_f32[:, 512:1024]]
        x2st = [Vsb[:, 4096:5120].bitcast(F32)[:, 0:512],
                Vsb[:, 5120:6144].bitcast(F32)[:, 0:512]]
        j5 = sin_sb[:, :].bitcast(BF16)[:, 512:1024]
        ln2T = AR_A
        x2row = [AR_C[:, 0:8192].bitcast(F32), AR_C[:, 8192:16384].bitcast(F32)]
        ln2bf = [Vsb[:, 0:4096], Vsb[:, 4096:8192]]
        xe2 = [kT_f32[:, 1024:1536], kT_f32[:, 1536:2048]]
        yst = [ropeA[:, :], ropeB[:, :]]

        sem_names = ["qS", "qO", "qG", "cT", "cV", "cS"]
        sems = {nm: e(nc.semaphore(nm)) for nm in sem_names}
        for nm in sem_names:
            P.n[nm] = 0

        def dma(eng, out, in_, q):
            v = P.bump(q, 16)
            P.emit(eng, lambda E, o=out, i=in_, s=q: E.dma_start(out=o, in_=i)
                   .then_inc(E._sems[s], 16))
            return v

        def mm(out, lhsT, rhs, start, stop, inc=False):
            if inc:
                v = P.bump("cT", 1)
                P.emit("tensor", lambda E, o=out, l=lhsT, r=rhs, a=start, b=stop:
                       E.matmul(o, l, r, start=a, stop=b, skip_group_check=True)
                       .then_inc(E._sems["cT"], 1))
                return v
            P.emit("tensor", lambda E, o=out, l=lhsT, r=rhs, a=start, b=stop:
                   E.matmul(o, l, r, start=a, stop=b, skip_group_check=True))
            return None

        def mm_group(psum_ap, lhs_of, rhs_of, n=32):
            for d in range(n):
                mm(psum_ap, lhs_of(d), rhs_of(d), d == 0, d == n - 1, inc=(d == n - 1))
            return P.n["cT"]

        def v_op(fn, inc=False):
            if inc:
                v = P.bump("cV", 1)
                P.emit("vector", lambda E, f=fn: f(E).then_inc(E._sems["cV"], 1))
                return v
            P.emit("vector", lambda E, f=fn: f(E))
            return None

        def s_op(fn, inc=False):
            if inc:
                v = P.bump("cS", 1)
                P.emit("scalar", lambda E, f=fn: f(E).then_inc(E._sems["cS"], 1))
                return v
            P.emit("scalar", lambda E, f=fn: f(E))
            return None

        def t_op(fn, inc=False):
            if inc:
                v = P.bump("cT", 1)
                P.emit("tensor", lambda E, f=fn: f(E).then_inc(E._sems["cT"], 1))
                return v
            P.emit("tensor", lambda E, f=fn: f(E))
            return None

        S1, S2, MV, TM, VV, YY, T2 = range(7)

        def sA(arr, b0, nb):
            return stats[:, arr * 8 + b0: arr * 8 + b0 + nb]

        def sA0(arr, b):
            return stats[:, arr * 8 + b: arr * 8 + b + 1]

        def ln_chain(nb, spacer_ap_a, spacer_ap_b, iters=3):
            def sp():
                v_op(lambda E: E.tensor_add(spacer_ap_a, spacer_ap_b,
                                            spacer_ap_b))

            v_op(lambda E: E.tensor_scalar_mul(sA(MV, 0, nb), sA(S1, 0, nb),
                                               -1.0 / D))
            steps = [
                lambda E: E.tensor_mul(sA(TM, 0, nb), sA(MV, 0, nb),
                                       sA(MV, 0, nb)),
                lambda E: E.scalar_tensor_tensor(
                    sA(VV, 0, nb), sA(S2, 0, nb), 1.0 / D, sA(TM, 0, nb),
                    ALU.mult, ALU.subtract),
                lambda E: E.tensor_scalar_add(sA(VV, 0, nb), sA(VV, 0, nb), EPS),
                lambda E: E.tensor_scalar_add(sA(T2, 0, nb), sA(VV, 0, nb), 1.0),
                lambda E: E.reciprocal(sA(YY, 0, nb), sA(T2, 0, nb)),
                lambda E: E.tensor_scalar_mul(sA(YY, 0, nb), sA(YY, 0, nb), 2.0),
            ]
            for _ in range(iters):
                steps += [
                    lambda E: E.tensor_mul(sA(TM, 0, nb), sA(YY, 0, nb),
                                           sA(YY, 0, nb)),
                    lambda E: E.tensor_mul(sA(TM, 0, nb), sA(TM, 0, nb),
                                           sA(VV, 0, nb)),
                    lambda E: E.tensor_scalar_mul(sA(TM, 0, nb), sA(TM, 0, nb),
                                                  -0.5),
                    lambda E: E.tensor_scalar_add(sA(TM, 0, nb), sA(TM, 0, nb),
                                                  1.5),
                    lambda E: E.tensor_mul(sA(YY, 0, nb), sA(YY, 0, nb),
                                           sA(TM, 0, nb)),
                ]
            for fn in steps:
                sp()
                v_op(fn)
            return v_op(lambda E: E.tensor_add(spacer_ap_a, spacer_ap_b,
                                               spacer_ap_b), inc=True)

        dma("sync", mask_sb[:], maskt, "qS")
        dma("sync", id_sb[:], identt, "qS")
        qS_pre = dma("sync", p64_sb[:], p64t, "qS")
        v_op(lambda E: E.memset(ones_bf[:], 1.0))
        v_op(lambda E: E.memset(ones_f[:], 1.0))
        v_op(lambda E: E.memset(stats[:], 1.0), inc=True)
        for a in range(4):
            dma("gpsimd",
                wk_bf.rearrange("p (a c) -> p a c", c=256)[:, 8 * a:8 * a + 8],
                wkt.rearrange("(a p) c -> p a c", p=128)[:, 8 * a:8 * a + 8], "qG")
        for a in range(4):
            dma("gpsimd",
                wv_bf.rearrange("p (a c) -> p a c", c=256)[:, 8 * a:8 * a + 8],
                wvt.rearrange("(a p) c -> p a c", p=128)[:, 8 * a:8 * a + 8], "qG")
        qG_wkv = P.n["qG"]

        snap_norm = {}
        snap_red = {}
        snap_tile_tr = {}
        snap_trgrp = {}
        snap_trev = {}
        snap_lnTmm = {}
        snap_spill = {}
        snap_lnTev = {}
        snap_rope = {}
        snap_vcp = {}
        snap_cos = {}
        snap_ropedone = {}

        gi_ctr = [0]
        ab_ctr = [0]
        TRB = [pbG, pbH]
        PAB = [pbA[:, :], pbB[:, :]]
        PEF = [pbE[:, :], pbF[:, :]]
        PCD = [pbC[:, :], pbD[:, :]]

        def transpose_group(src_of, dest_ap, n=8):
            gi = gi_ctr[0]
            bank = TRB[gi % 2]
            if gi - 2 in snap_trev:
                P.wait("tensor", "cV", snap_trev[gi - 2])
            for j in range(n):
                if j == n - 1:
                    snap_trgrp[gi] = P.bump("cT", 1)
                    P.emit("tensor", lambda E, o=bank[:, j * 128:(j + 1) * 128],
                           i=src_of(j): E.transpose(o, i, id_sb[:])
                           .then_inc(E._sems["cT"], 1))
                else:
                    P.emit("tensor", lambda E, o=bank[:, j * 128:(j + 1) * 128],
                           i=src_of(j): E.transpose(o, i, id_sb[:]))
            P.wait("vector", "cT", snap_trgrp[gi])
            src3 = bank[:, 0:n * 128].rearrange("p (a t) -> p a t", t=128)
            snap_trev[gi] = v_op(
                lambda E, d=dest_ap, s=src3: E.tensor_copy(out=d, in_=s), inc=True)
            gi_ctr[0] += 1
            return snap_trev[gi]

        def rope_emit(proj_psum, dest_ap):
            ab = ab_ctr[0]
            cp = v_op(lambda E, p=proj_psum: E.tensor_copy(out=qrot[:], in_=p),
                      inc=True)
            rb = PEF[ab % 2]
            P.wait("tensor", "cV", cp)
            if ab - 2 in snap_rope:
                P.wait("tensor", "cV", snap_rope[ab - 2])
            rt = t_op(lambda E, r=rb: E.matmul(
                r, p64_sb[:], qrot[:], start=True, stop=True,
                skip_group_check=True), inc=True)
            P.wait("vector", "cT", rt)
            v_op(lambda E, p=proj_psum: E.tensor_mul(ropeA[:], p, cos_sb[:]))
            v_op(lambda E, r=rb: E.tensor_mul(ropeB[:], r, sin_sb[:]))
            snap_rope[ab] = v_op(
                lambda E, d=dest_ap: E.tensor_add(d, ropeA[:], ropeB[:]), inc=True)
            ab_ctr[0] += 1
            return snap_rope[ab]

        for c in range(8):
            if c - 1 in snap_ropedone:
                P.wait("scalar", "cV", snap_ropedone[c - 1])
            dma("scalar", cos_sb[:], cost[:, 512 * c:512 * (c + 1)], "qO")
            snap_cos[c] = dma("scalar", sin_sb[:], sint[:, 512 * c:512 * (c + 1)], "qO")

            for st in range(4):
                ti = 4 * c + st
                if st >= 2:
                    P.wait("sync", "cV", snap_red[ti - 2])
                elif c >= 1:
                    P.wait("sync", "cV", snap_norm[4 * c - 2 + st])
                qs_x = dma("sync", xbuf[st % 2][:, :],
                           xb[128 * ti:128 * (ti + 1), :], "qS")
                P.wait("vector", "qS", qs_x)
                xt = xbuf[st % 2][:, :]
                v_op(lambda E, x=xt, o=sA0(S1, st): E.reduce_sum(o, x, axis=AX.X))
                snap_red[ti] = v_op(
                    lambda E, x=xt, o=sA0(S2, st): E.scalar_tensor_tensor(
                        junk[:], x, 1.0, x, ALU.mult, ALU.mult, accum_out=o),
                    inc=True)
            jt = junk.rearrange("p (a t) -> p a t", t=1024)
            ln_chain(4, jt[:, 2, :], jt[:, 3, :])
            for st in range(4):
                ti = 4 * c + st
                if st >= 2:
                    P.wait("sync", "cV", snap_norm[ti - 2])
                else:
                    P.wait("sync", "cV", snap_red[4 * c + 2 + st])
                qs_x = dma("sync", xbuf[st % 2][:, :],
                           xb[128 * ti:128 * (ti + 1), :], "qS")
                P.wait("vector", "qS", qs_x)
                xt = xbuf[st % 2][:, :]
                if ti - 2 in snap_tile_tr:
                    P.wait("vector", "cT", snap_tile_tr[ti - 2])
                snap_norm[ti] = v_op(
                    lambda E, x=xt, o=lnbf[st % 2][:, :], s_=st:
                    E.tensor_scalar(o, x, sA0(MV, s_), sA0(YY, s_),
                                    ALU.add, ALU.mult), inc=True)
                P.wait("tensor", "cV", snap_norm[ti])
                if ti == 0:
                    P.wait("tensor", "qS", qS_pre)
                for grp in range(4):
                    if st == 0 and grp == 0:
                        if c - 2 in snap_lnTmm:
                            P.wait("vector", "cT", snap_lnTmm[c - 2])
                            P.wait("vector", "qO", snap_spill[c - 2])
                    src = lnbf[st % 2]
                    dst = lnT[c % 2].rearrange("p (a t) -> p a t", t=512)[
                        :, 8 * grp:8 * grp + 8, st * 128:(st + 1) * 128]
                    transpose_group(
                        lambda j, s=src, g=grp:
                        s[:, (8 * g + j) * 128:(8 * g + j + 1) * 128], dst)
                snap_tile_tr[ti] = P.n["cT"]
            snap_lnTev[c] = P.n["cV"]

            lnc = lnT[c % 2].rearrange("p (a t) -> p a t", t=512)
            if c == 0:
                P.wait("tensor", "qG", qG_wkv)
            for kk in range(NKV):
                bank = PAB[ab_ctr[0] % 2]
                if ab_ctr[0] - 2 in snap_rope:
                    P.wait("tensor", "cV", snap_rope[ab_ctr[0] - 2])
                P.wait("tensor", "cV", snap_lnTev[c])
                kmm = mm_group(
                    bank,
                    lambda d, k=kk: wk_bf[:, d * 256 + k * 128: d * 256 + (k + 1) * 128],
                    lambda d: lnc[:, d, :])
                P.wait("vector", "cT", kmm)
                P.wait("vector", "qO", snap_cos[c])
                rope_emit(bank, kT[:, kk * T + 512 * c: kk * T + 512 * (c + 1)])
            snap_ropedone[c] = P.n["cV"]
            for vv in range(NKV):
                bank = PCD[vv]
                if (c - 1, vv) in snap_vcp:
                    P.wait("tensor", "cV", snap_vcp[(c - 1, vv)])
                vmm = mm_group(
                    bank,
                    lambda d, k=vv: wv_bf[:, d * 256 + k * 128: d * 256 + (k + 1) * 128],
                    lambda d: lnc[:, d, :])
                P.wait("vector", "cT", vmm)
                snap_vcp[(c, vv)] = v_op(
                    lambda E, b=bank, o=vtst[:, vv * 512:(vv + 1) * 512]:
                    E.tensor_copy(out=o, in_=b), inc=True)
            P.wait("tensor", "cV", snap_vcp[(c, 1)])
            transpose_group(
                lambda j: vtst[:, (j % 2) * 512 + (j // 2) * 128:
                               (j % 2) * 512 + (j // 2) * 128 + 128],
                Vsb[:, c * 1024:(c + 1) * 1024].rearrange(
                    "p (a t) -> p a t", t=128))
            snap_lnTmm[c] = P.n["cT"]
            P.wait("scalar", "cV", snap_lnTev[c])
            for a in range(4):
                dma("scalar",
                    lnT_sc.rearrange("(a p) t -> p a t", p=128)[
                        :, 8 * a:8 * a + 8, 512 * c:512 * (c + 1)],
                    lnT[c % 2].rearrange("p (a t) -> p a t", t=512)[
                        :, 8 * a:8 * a + 8, :], "qO")
            snap_spill[c] = P.n["qO"]

        cT_P1 = P.n["cT"]
        cV_P1 = P.n["cV"]
        cS_P1 = P.n["cS"]

        snap_p2mm = {}
        for qp in range(2):
            if qp == 0:
                P.wait("gpsimd", "cT", cT_P1)
            else:
                P.wait("gpsimd", "cT", snap_p2mm[(0, 7)])
            for a in range(4):
                dma("gpsimd",
                    wq_bf.rearrange("p (a c) -> p a c", c=512)[:, 8 * a:8 * a + 8],
                    wqt.rearrange("(a p) c -> p a c", p=128)[
                        :, 8 * a:8 * a + 8, 512 * qp:512 * (qp + 1)], "qG")
            qG_wq = P.n["qG"]
            for c in range(8):
                if qp == 0 and c < 2:
                    P.wait("sync", "cT", cT_P1)
                    P.wait("sync", "qO", snap_spill[7])
                prev = (qp, c - 2) if c >= 2 else ((qp - 1, 6 + c) if qp else None)
                if prev in snap_p2mm:
                    P.wait("sync", "cT", snap_p2mm[prev])
                for a in range(4):
                    dma("sync",
                        lnT[c % 2].rearrange("p (a t) -> p a t", t=512)[
                            :, 8 * a:8 * a + 8, :],
                        lnT_sc.rearrange("(a p) t -> p a t", p=128)[
                            :, 8 * a:8 * a + 8, 512 * c:512 * (c + 1)], "qS")
                qs_ln = P.n["qS"]
                key_prev = ("p2", qp, c - 1) if c else (
                    ("p2", 0, 7) if qp else 7)
                if key_prev in snap_ropedone:
                    P.wait("scalar", "cV", snap_ropedone[key_prev])
                dma("scalar", cos_sb[:], cost[:, 512 * c:512 * (c + 1)], "qO")
                sc_cos = dma("scalar", sin_sb[:], sint[:, 512 * c:512 * (c + 1)], "qO")
                lnc = lnT[c % 2].rearrange("p (a t) -> p a t", t=512)
                for hh in range(4):
                    H = 4 * qp + hh
                    bank = PAB[ab_ctr[0] % 2]
                    if ab_ctr[0] - 2 in snap_rope:
                        P.wait("tensor", "cV", snap_rope[ab_ctr[0] - 2])
                    P.wait("tensor", "qS", qs_ln)
                    if hh == 0 and c == 0:
                        P.wait("tensor", "qG", qG_wq)
                    qmm = mm_group(
                        bank,
                        lambda d, h=hh: wq_bf[:, d * 512 + h * 128:
                                              d * 512 + (h + 1) * 128],
                        lambda d: lnc[:, d, :])
                    P.wait("vector", "cT", qmm)
                    P.wait("vector", "qO", sc_cos)
                    if qp == 0 and c == 0 and hh == 0:
                        P.wait("vector", "cT", cT_P1)
                    rope_emit(bank, qT[:, H * T + 512 * c: H * T + 512 * (c + 1)])
                snap_ropedone[("p2", qp, c)] = P.n["cV"]
                snap_p2mm[(qp, c)] = P.n["cT"]

        cT_P2 = P.n["cT"]
        cV_P2 = P.n["cV"]

        if dbg:
            P.wait("sync", "cV", cV_P2)
            dma("sync", d_kT, kT[:, :], "qS")
            dma("sync", d_V, Vsb[:, :], "qS")
            dma("sync", d_qT, qT[:, :], "qS")

        snap_exp = {}
        snap_mask = {}
        snap_ctxmm = {}
        snap_iter = {}
        snap_ctxdma = {}
        kglob = 0
        it_ctr = 0
        h_end_cT = {}
        ag_qO = {}
        RB = pbG[:, :].bitcast(F32)

        for h in range(NKV):
            for s in range(8):
                for g in range(4):
                    H = 4 * h + g
                    nk = 4 * s + 4
                    su = PEF[it_ctr % 2]
                    cx = PCD[it_ctr % 2]
                    if it_ctr - 2 in snap_iter:
                        P.wait("tensor", "cV", snap_iter[it_ctr - 2])
                    for k in range(nk):
                        sb_ = PAB[kglob % 2]
                        eb = EB[kglob % 3]
                        if kglob - 2 in snap_exp:
                            P.wait("tensor", "cS", snap_exp[kglob - 2])
                        if kglob == 0:
                            P.wait("tensor", "cV", cV_P2)
                        smm = mm(sb_,
                                 kT[:, h * T + 128 * k: h * T + 128 * (k + 1)],
                                 qT[:, H * T + 512 * s: H * T + 512 * (s + 1)],
                                 True, True, inc=True)
                        if kglob - 3 in snap_ctxmm:
                            P.wait("scalar", "cT", snap_ctxmm[kglob - 3])
                        P.wait("scalar", "cT", smm)
                        if kglob == 0:
                            P.wait("scalar", "cV", cV_P2)
                        snap_exp[kglob] = s_op(
                            lambda E, p=sb_, o=eb: E.activation(
                                o, p, AF.Exp, scale=ISQ), inc=True)
                        if k >= 4 * s:
                            off = 384 - (128 * k - 512 * s)
                            P.wait("vector", "cS", snap_exp[kglob])
                            snap_mask[kglob] = v_op(
                                lambda E, o=eb, m=mask_sb[:, off:off + 512]:
                                E.tensor_tensor(out=o, in0=o, in1=m, op=ALU.mult),
                                inc=True)
                            P.wait("tensor", "cV", snap_mask[kglob])
                        else:
                            P.wait("tensor", "cS", snap_exp[kglob])
                        mm(su, ones_bf[:, :], eb, k == 0, k == nk - 1)
                        snap_ctxmm[kglob] = mm(
                            cx, Vsb[:, k * 256 + h * 128: k * 256 + (h + 1) * 128],
                            eb, k == 0, k == nk - 1, inc=True)
                        kglob += 1
                    P.wait("vector", "cT", snap_ctxmm[kglob - 1])
                    rec = ropeB[0:1, :]
                    vrec = v_op(lambda E, s_=su: E.reciprocal(rec, s_[0:1, :]),
                                inc=True)
                    P.wait("tensor", "cV", vrec)
                    romm = t_op(lambda E: E.matmul(
                        RB, ones_f[:], rec, start=True, stop=True,
                        skip_group_check=True), inc=True)
                    P.wait("vector", "cT", romm)
                    v_op(lambda E: E.tensor_copy(out=ropeA[:], in_=RB))
                    if it_ctr - 2 in snap_ctxdma:
                        P.wait("vector", "qO", snap_ctxdma[it_ctr - 2])
                    cst = ctxst[it_ctr % 2]
                    vdone = v_op(lambda E, c_=cx, o=cst: E.tensor_tensor(
                        out=o, in0=c_, in1=ropeA[:], op=ALU.mult), inc=True)
                    snap_iter[it_ctr] = vdone
                    P.wait("scalar", "cV", vdone)
                    snap_ctxdma[it_ctr] = dma(
                        "scalar",
                        cc_in[h][s // 2][128 * g:128 * (g + 1),
                                         512 * (s % 2):512 * (s % 2 + 1)],
                        cst, "qO")
                    it_ctr += 1
                if s % 2 == 1:
                    ag_qO[(h, s // 2)] = P.n["qO"]
            h_end_cT[h] = P.n["cT"]

        def emit_ag(h, qt):
            P.wait("gpsimd", "qO", ag_qO[(h, qt)])
            v = P.bump("qG", 1)
            P.emit("gpsimd", lambda E, hh=h, q=qt: E.collective_compute(
                "AllGather", ALU.bypass,
                replica_groups=[[0, 1, 2, 3], [4, 5, 6, 7]],
                ins=[cc_in[hh][q]],
                outs=[cc_out[hh][q]]).then_inc(E._sems["qG"], 1))
            return v

        def emit_switch_loads(h):
            for _ in range(16):
                P.bump("qG", 16)

            def do_switch(E, hh=h):
                pid = E.partition_id()
                qv = E.compute_val(pid % 4)
                for body in nc.Switch(engines=[E], index=[qv], n=4):
                    for blk in range(16):
                        E.dma_start(
                            out=ctxF[:, (hh * 16 + blk) * 1024:
                                     (hh * 16 + blk + 1) * 1024],
                            in_=cc_out[hh][body][blk * 128:(blk + 1) * 128, :],
                        ).then_inc(E._sems["qG"], 16)
            P.emit("gpsimd", do_switch)
            return P.n["qG"]

        def emit_wo_cast(e_, wait_cT):
            P.wait("gpsimd", "cT", wait_cT)
            for a in range(4):
                dma("gpsimd",
                    wo_bf[e_ % 2].rearrange("p (a c) -> p a c", c=512)[
                        :, 8 * a:8 * a + 8],
                    wot.rearrange("(a p) c -> p a c", p=128)[
                        :, 8 * a:8 * a + 8, 512 * e_:512 * (e_ + 1)], "qG")
            return P.n["qG"]

        for qt in range(4):
            emit_ag(0, qt)
        ag0_last = P.n["qG"]
        P.wait("gpsimd", "cT", cT_P2)
        P.wait("gpsimd", "qG", ag0_last)
        qG_ld0 = emit_switch_loads(0)
        emit_ag(1, 0)
        emit_ag(1, 1)
        snap_wocast = {0: emit_wo_cast(0, h_end_cT[0])}
        emit_ag(1, 2)
        emit_ag(1, 3)
        P.wait("gpsimd", "qG", P.n["qG"])
        qG_ld1 = emit_switch_loads(1)
        snap_wocast[1] = emit_wo_cast(1, h_end_cT[1])

        snap_p5mm = {}
        snap_evac5 = {}
        snap_x2dma = {}
        ctxFv = ctxF.rearrange("p (a t) -> p a t", t=1024)

        if dbg:
            P.wait("sync", "qG", qG_ld1)
            dma("sync", d_ctxF, ctxF[:, :], "qS")
            P.wait("sync", "qG", snap_wocast[0])
            dma("sync", d_wo0, wo_bf[0], "qS")

        xeC = kT_f32[:, :].rearrange("p (a t) -> p a t", t=512)
        snap_xeC = {}
        tt_ctr = 0
        for e_ in range(8):
            wob = wo_bf[e_ % 2].rearrange("p (a c) -> p a c", c=512)
            if e_ == 0:
                P.wait("sync", "cT", h_end_cT[1])
            else:
                P.wait("sync", "cV", snap_evac5[8 * e_ - 1])
            snap_xeC[e_] = dma(
                "sync", xeC,
                xq.rearrange("(a p) d -> p a d", p=128)[:, :,
                                                        512 * e_:512 * (e_ + 1)],
                "qS")
            for tt in range(8):
                bank = PAB[tt_ctr % 2]
                if tt_ctr - 2 in snap_evac5:
                    P.wait("tensor", "cV", snap_evac5[tt_ctr - 2])
                if tt == 0:
                    P.wait("tensor", "qG", snap_wocast[e_])
                    if e_ == 0:
                        P.wait("tensor", "qG", qG_ld1)
                xet = xeC[:, tt, :]
                qs_xe = snap_xeC[e_]
                pm = mm_group(bank,
                              lambda d, t_=tt: ctxFv[:, d, t_ * 128:(t_ + 1) * 128],
                              lambda d: wob[:, d, :])
                snap_p5mm[(e_, tt)] = pm
                if dbg and e_ == 0 and tt == 0:
                    P.wait("vector", "cT", pm)
                    vm5 = v_op(lambda E, p=bank: E.tensor_copy(
                        out=ropeA[:], in_=p), inc=True)
                    P.wait("scalar", "cV", vm5)
                    qp5 = dma("scalar", d_op, ropeA[:], "qO")
                    P.wait("vector", "qO", qp5)
                P.wait("vector", "cT", pm)
                P.wait("vector", "qS", qs_xe)
                if tt_ctr - 2 in snap_x2dma:
                    P.wait("vector", "qO", snap_x2dma[tt_ctr - 2])
                x2s = x2st[tt_ctr % 2]
                c1 = p5st[:, tt * 16 + e_: tt * 16 + e_ + 1]
                c2 = p5st[:, tt * 16 + 8 + e_: tt * 16 + 8 + e_ + 1]
                v_op(lambda E, b=bank, x=xet, o=x2s, a=c1: E.scalar_tensor_tensor(
                    o, b, 1.0, x, ALU.mult, ALU.add, accum_out=a))
                j5 = Vsb[:, 2048:2560]
                snap_evac5[tt_ctr] = v_op(
                    lambda E, o=j5, x=x2s, a=c2: E.scalar_tensor_tensor(
                        o, x, 1.0, x, ALU.mult, ALU.mult, accum_out=a), inc=True)
                P.wait("scalar", "cV", snap_evac5[tt_ctr])
                snap_x2dma[tt_ctr] = dma(
                    "scalar",
                    x2_sc[128 * tt:128 * (tt + 1), 512 * e_:512 * (e_ + 1)],
                    x2s, "qO")
                tt_ctr += 1
            if e_ + 2 <= 7:
                snap_wocast[e_ + 2] = emit_wo_cast(e_ + 2, snap_p5mm[(e_, 7)])

        cV_P5 = P.n["cV"]
        qO_P5 = P.n["qO"]

        snap_n2 = {}
        snap_tr2t = {}
        for tt in range(8):
            v_op(lambda E, o=sA0(S1, tt), i=p5st[:, tt * 16:tt * 16 + 8]:
                 E.reduce_sum(o, i, axis=AX.X))
            v_op(lambda E, o=sA0(S2, tt), i=p5st[:, tt * 16 + 8:tt * 16 + 16]:
                 E.reduce_sum(o, i, axis=AX.X))
        ln_chain(8, kT_f32[:, 2048:2560], kT_f32[:, 2560:3072], iters=5)
        for tt in range(8):
            par = tt % 2
            if tt - 2 in snap_n2:
                P.wait("sync", "cV", snap_n2[tt - 2])
            if tt == 0:
                P.wait("sync", "qO", qO_P5)
            qs_x2 = dma("sync", x2row[par][:, :],
                        x2_sc[128 * tt:128 * (tt + 1), :], "qS")
            P.wait("vector", "qS", qs_x2)
            if tt - 2 in snap_tr2t:
                P.wait("vector", "cT", snap_tr2t[tt - 2])
            snap_n2[tt] = v_op(
                lambda E, x=x2row[par][:, :], o=ln2bf[par][:, :], t_=tt:
                E.tensor_scalar(o, x, sA0(MV, t_), sA0(YY, t_),
                                ALU.add, ALU.mult), inc=True)
            P.wait("tensor", "cV", snap_n2[tt])
            if tt == 0:
                P.wait("tensor", "cV", cV_P5)
            for grp in range(4):
                src = ln2bf[par]
                dst = ln2T.rearrange("p (a t) -> p a t", t=1024)[
                    :, 8 * grp:8 * grp + 8, tt * 128:(tt + 1) * 128]
                transpose_group(
                    lambda j, s=src, g=grp:
                    s[:, (8 * g + j) * 128:(8 * g + j + 1) * 128], dst)
            snap_tr2t[tt] = P.n["cT"]
        cV_ln2 = P.n["cV"]

        snap_itcast = {}
        snap_f6mm = {}
        snap_yev = {}
        snap_ydma = {}

        def emit_it_cast(e_, wait_cT):
            P.wait("gpsimd", "cT", wait_cT)
            for a in range(4):
                dma("gpsimd",
                    it_bf[e_ % 2].rearrange("p (a c) -> p a c", c=512)[
                        :, 8 * a:8 * a + 8],
                    itt.rearrange("(a p) c -> p a c", p=128)[
                        :, 8 * a:8 * a + 8, 512 * e_:512 * (e_ + 1)], "qG")
            return P.n["qG"]

        snap_itcast[0] = emit_it_cast(0, snap_p5mm[(6, 7)])
        snap_itcast[1] = emit_it_cast(1, snap_p5mm[(7, 7)])

        ln2Tv = ln2T.rearrange("p (a t) -> p a t", t=1024)
        tt6 = 0
        for e_ in range(8):
            itb = it_bf[e_ % 2].rearrange("p (a c) -> p a c", c=512)
            for tt in range(8):
                bank = PAB[tt6 % 2]
                if tt6 - 2 in snap_yev:
                    P.wait("tensor", "cV", snap_yev[tt6 - 2])
                if tt == 0:
                    P.wait("tensor", "qG", snap_itcast[e_])
                    if e_ == 0:
                        P.wait("tensor", "cV", cV_ln2)
                xet2 = xe2[tt6 % 2]
                if tt6 - 2 in snap_yev:
                    P.wait("sync", "cV", snap_yev[tt6 - 2])
                qs_xe2 = dma(
                    "sync", xet2,
                    x2_sc[128 * tt:128 * (tt + 1), 512 * e_:512 * (e_ + 1)], "qS")
                fm = mm_group(bank,
                              lambda d, t_=tt: ln2Tv[:, d, t_ * 128:(t_ + 1) * 128],
                              lambda d: itb[:, d, :])
                snap_f6mm[(e_, tt)] = fm
                P.wait("vector", "cT", fm)
                P.wait("vector", "qS", qs_xe2)
                if tt6 - 2 in snap_ydma:
                    P.wait("vector", "qO", snap_ydma[tt6 - 2])
                yb = yst[tt6 % 2]
                snap_yev[tt6] = v_op(
                    lambda E, b=bank, x=xet2, o=yb: E.tensor_add(o, b, x), inc=True)
                P.wait("scalar", "cV", snap_yev[tt6])
                snap_ydma[tt6] = dma(
                    "scalar", y[128 * tt:128 * (tt + 1), 512 * e_:512 * (e_ + 1)],
                    yb, "qO")
                tt6 += 1
            if e_ + 2 <= 7:
                snap_itcast[e_ + 2] = emit_it_cast(e_ + 2, snap_f6mm[(e_, 7)])

        if dbg:
            P.wait("sync", "qG", P.n["qG"])
            P.wait("sync", "qO", P.n["qO"])
            for h in range(NKV):
                for qt in range(4):
                    dma("sync", d_cc[h][:, 1024 * qt:1024 * (qt + 1)],
                        cc_out[h][qt], "qS")
            dma("sync", d_lnT, lnT_sc, "qS")
            dma("sync", d_x2, x2_sc, "qS")
            dma("sync", d_acc[:, 0:16], p5st[:, 0:16], "qS")

        P.wait("sync", "qO", P.n["qO"])
        P.wait("sync", "qG", P.n["qG"])
        P.wait("sync", "qS", P.n["qS"])

        with nc.Block() as block:

            def run(E, eng):
                E._sems = sems
                for fn in P.ops[eng]:
                    fn(E)

            @block.sync
            def _(E):
                run(E, "sync")

            @block.gpsimd
            def _(E):
                run(E, "gpsimd")

            @block.scalar
            def _(E):
                run(E, "scalar")

            @block.vector
            def _(E):
                run(E, "vector")

            @block.tensor
            def _(E):
                run(E, "tensor")

    return nc


def _rope_tables():
    ROPE_THETA = 500000.0
    ROPE_FACTOR = 32.0
    HI, LO = 4.0, 1.0
    ORIG = 8192
    d = np.arange(0, DH, 2, dtype=np.float64) / DH
    inv = 1.0 / ROPE_THETA ** d
    wav = 2.0 * math.pi / inv
    cut_lo = ORIG / HI
    cut_hi = ORIG / LO
    inv = np.where(wav > cut_hi, inv / ROPE_FACTOR, inv)
    smooth = (ORIG / wav - LO) / (HI - LO)
    smoothed = (1.0 - smooth) * inv / ROPE_FACTOR + smooth * inv
    med = (~(wav < cut_hi)) & (~(wav > cut_lo))
    inv = np.where(med, smoothed, inv)
    pos = np.arange(T, dtype=np.float64)
    freqs = pos[:, None] * inv[None, :]
    emb = np.concatenate([freqs, freqs], axis=1)
    cos = np.ascontiguousarray(np.cos(emb).T.astype(np.float32))
    sin = np.sin(emb).T.astype(np.float32)
    sin_signed = sin.copy()
    sin_signed[0:64] = -sin[0:64]
    return cos, np.ascontiguousarray(sin_signed)


def kernel(L0_attn_x, L0_attn_Wq, L0_attn_Wk, L0_attn_Wv, L0_attn_Wo,
           L0_ffn_identity):
    x = np.asarray(L0_attn_x, dtype=np.float32)
    Wq = np.asarray(L0_attn_Wq, dtype=np.float32)
    Wk = np.asarray(L0_attn_Wk, dtype=np.float32)
    Wv = np.asarray(L0_attn_Wv, dtype=np.float32)
    Wo = np.asarray(L0_attn_Wo, dtype=np.float32)
    I = np.asarray(L0_ffn_identity, dtype=np.float32)

    cosT, sinT = _rope_tables()
    mask = np.zeros((128, 896), dtype=ml_dtypes.bfloat16)
    for p in range(128):
        mask[p, p + 384:] = 1.0
    ident = np.eye(128, dtype=ml_dtypes.bfloat16)
    p64 = np.zeros((128, 128), dtype=ml_dtypes.bfloat16)
    for i in range(128):
        p64[i, i ^ 64] = 1.0

    perm = []
    for h in range(2):
        for r in range(4):
            for g in range(4):
                head = 8 * r + 4 * h + g
                perm.extend(range(head * 128, head * 128 + 128))
    WoT_perm = np.ascontiguousarray(Wo.T[np.array(perm)])
    IT = np.ascontiguousarray(I.T)

    in_maps = []
    for c in range(8):
        b, qq = c // 4, c % 4
        in_maps.append({
            "xb": np.ascontiguousarray(x[b]),
            "xq": np.ascontiguousarray(x[b, 1024 * qq:1024 * (qq + 1)]),
            "wqt": np.ascontiguousarray(Wq[1024 * qq:1024 * (qq + 1), :].T),
            "wkt": np.ascontiguousarray(Wk[256 * qq:256 * (qq + 1), :].T),
            "wvt": np.ascontiguousarray(Wv[256 * qq:256 * (qq + 1), :].T),
            "wot": WoT_perm,
            "itt": IT,
            "cost": cosT,
            "sint": sinT,
            "maskt": mask,
            "identt": ident,
            "p64t": p64,
        })

    nc = build_nc()
    res = run_bass_kernel_spmd(nc, in_maps, core_ids=list(range(8)))
    out = np.empty((2, T, D), dtype=np.float32)
    for c in range(8):
        b, qq = c // 4, c % 4
        out[b, 1024 * qq:1024 * (qq + 1)] = res.results[c]["y"]
    return out


# version 08
# speedup vs baseline: 1.0925x; 1.0448x over previous
import math
from contextlib import ExitStack

import numpy as np
import ml_dtypes

import concourse.bass as bass
import concourse.mybir as mybir
from concourse.bass_utils import run_bass_kernel_spmd

F32 = mybir.dt.float32
BF16 = mybir.dt.bfloat16
AF = mybir.ActivationFunctionType
ALU = mybir.AluOpType
AX = mybir.AxisListType

T = 4096
D = 4096
DH = 128
NH = 8
NKV = 2
QR = 1024
EPS = 1e-5
ISQ = 1.0 / math.sqrt(128.0)

ENGS = ("sync", "gpsimd", "scalar", "vector", "tensor")


class Prog:
    def __init__(self):
        self.ops = {e: [] for e in ENGS}
        self.n = {}

    def emit(self, eng, fn):
        self.ops[eng].append(fn)

    def wait(self, eng, sem, val):
        if val and val > 0:
            self.emit(eng, lambda E, s=sem, v=val: E.wait_ge(E._sems[s], v))

    def bump(self, sem, amt):
        self.n[sem] = self.n.get(sem, 0) + amt
        return self.n[sem]


def build_nc(dbg=False):
    nc = bass.Bass("TRN2", target_bir_lowering=False)

    xb = nc.dram_tensor("xb", [T, D], F32, kind="ExternalInput").ap()
    xq = nc.dram_tensor("xq", [QR, D], F32, kind="ExternalInput").ap()
    wqt = nc.dram_tensor("wqt", [D, NH * DH], F32, kind="ExternalInput").ap()
    wkt = nc.dram_tensor("wkt", [D, NKV * DH], F32, kind="ExternalInput").ap()
    wvt = nc.dram_tensor("wvt", [D, NKV * DH], F32, kind="ExternalInput").ap()
    wot = nc.dram_tensor("wot", [D, D], F32, kind="ExternalInput").ap()
    itt = nc.dram_tensor("itt", [D, D], F32, kind="ExternalInput").ap()
    cost = nc.dram_tensor("cost", [DH, T], F32, kind="ExternalInput").ap()
    sint = nc.dram_tensor("sint", [DH, T], F32, kind="ExternalInput").ap()
    maskt = nc.dram_tensor("maskt", [128, 896], BF16, kind="ExternalInput").ap()
    identt = nc.dram_tensor("identt", [128, 128], BF16, kind="ExternalInput").ap()
    p64t = nc.dram_tensor("p64t", [128, 128], BF16, kind="ExternalInput").ap()
    y = nc.dram_tensor("y", [QR, D], F32, kind="ExternalOutput").ap()

    lnT_sc = nc.dram_tensor("lnT_sc", [D, T], BF16, kind="Internal").ap()
    x2_sc = nc.dram_tensor("x2_sc", [QR, D], F32, kind="Internal").ap()
    cc_in = [[nc.dram_tensor(f"cc_in{h}_{qt}", [4 * DH, 1024], BF16,
                             kind="Internal").ap() for qt in range(4)]
             for h in range(NKV)]
    cc_out = [[nc.dram_tensor(f"cc_out{h}_{qt}", [16 * DH, 1024], BF16,
                              kind="Internal").ap() for qt in range(4)]
              for h in range(NKV)]

    if dbg:
        d_lnT = nc.dram_tensor("d_lnT", [D, T], BF16, kind="ExternalOutput").ap()
        d_kT = nc.dram_tensor("d_kT", [128, 8192], BF16, kind="ExternalOutput").ap()
        d_V = nc.dram_tensor("d_V", [128, 8192], BF16, kind="ExternalOutput").ap()
        d_qT = nc.dram_tensor("d_qT", [128, 32768], BF16, kind="ExternalOutput").ap()
        d_cc = [nc.dram_tensor(f"d_cc{h}", [16 * DH, T], BF16, kind="ExternalOutput").ap()
                for h in range(NKV)]
        d_x2 = nc.dram_tensor("d_x2", [QR, D], F32, kind="ExternalOutput").ap()
        d_ctxF = nc.dram_tensor("d_ctxF", [128, 32768], BF16, kind="ExternalOutput").ap()
        d_op = nc.dram_tensor("d_op", [128, 512], F32, kind="ExternalOutput").ap()
        d_wo0 = nc.dram_tensor("d_wo0", [128, 16384], BF16, kind="ExternalOutput").ap()
        d_acc = nc.dram_tensor("d_acc", [128, 16], F32, kind="ExternalOutput").ap()

    P = Prog()

    with ExitStack() as ctx:
        e = ctx.enter_context

        AR_A = e(nc.sbuf_tensor("AR_A", [128, 32768], BF16))
        AR_B = e(nc.sbuf_tensor("AR_B", [128, 32768], BF16))
        AR_C = e(nc.sbuf_tensor("AR_C", [128, 16384], BF16))
        kT = e(nc.sbuf_tensor("kT", [128, NKV * T], BF16))
        Vsb = e(nc.sbuf_tensor("Vsb", [128, 8192], BF16))
        cos_sb = e(nc.sbuf_tensor("cos_sb", [128, 512], F32))
        sin_sb = e(nc.sbuf_tensor("sin_sb", [128, 512], F32))
        mask_sb = e(nc.sbuf_tensor("mask_sb", [128, 896], BF16))
        id_sb = e(nc.sbuf_tensor("id_sb", [128, 128], BF16))
        p64_sb = e(nc.sbuf_tensor("p64_sb", [128, 128], BF16))
        ones_bf = e(nc.sbuf_tensor("ones_bf", [128, 128], BF16))
        ones_f = e(nc.sbuf_tensor("ones_f", [1, 128], F32))
        ropeA = e(nc.sbuf_tensor("ropeA", [128, 512], F32))
        ropeB = e(nc.sbuf_tensor("ropeB", [128, 512], F32))
        qrot = e(nc.sbuf_tensor("qrot", [128, 512], BF16))
        vtst = e(nc.sbuf_tensor("vtst", [128, 1024], BF16))
        stats = e(nc.sbuf_tensor("stats", [128, 64], F32))
        p5st = e(nc.sbuf_tensor("p5st", [128, 128], F32))

        pbA = e(nc.psum_tensor("pbA", [128, 512], F32))
        pbB = e(nc.psum_tensor("pbB", [128, 512], F32))
        pbC = e(nc.psum_tensor("pbC", [128, 512], F32))
        pbD = e(nc.psum_tensor("pbD", [128, 512], F32))
        pbE = e(nc.psum_tensor("pbE", [128, 512], F32))
        pbF = e(nc.psum_tensor("pbF", [128, 512], F32))
        pbG = e(nc.psum_tensor("pbG", [128, 1024], BF16))
        pbH = e(nc.psum_tensor("pbH", [128, 1024], BF16))

        lnT = [AR_A[:, 0:16384], AR_A[:, 16384:32768]]
        xbuf = [AR_B[:, 0:8192].bitcast(F32), AR_B[:, 8192:16384].bitcast(F32)]
        lnbf = [AR_B[:, 16384:20480], AR_B[:, 20480:24576]]
        junk = AR_B[:, 24576:28672]
        wk_bf = AR_C[:, 0:8192]
        wv_bf = AR_C[:, 8192:16384]
        wq_bf = AR_C[:, 0:16384]
        qT = AR_B
        EB = [cos_sb[:, :].bitcast(BF16)[:, 0:512],
              cos_sb[:, :].bitcast(BF16)[:, 512:1024],
              sin_sb[:, :].bitcast(BF16)[:, 0:512],
              sin_sb[:, :].bitcast(BF16)[:, 512:1024]]
        ctxst = [vtst[:, 0:512], vtst[:, 512:1024]]
        ctxF = AR_A
        wo_bf = [AR_B[:, 0:16384], AR_B[:, 16384:32768]]
        it_bf = wo_bf
        kT_f32 = kT[:, :].bitcast(F32)
        xe_t = [kT_f32[:, 0:512], kT_f32[:, 512:1024]]
        x2st = [Vsb[:, 4096:5120].bitcast(F32)[:, 0:512],
                Vsb[:, 5120:6144].bitcast(F32)[:, 0:512]]
        j5 = sin_sb[:, :].bitcast(BF16)[:, 512:1024]
        ln2T = AR_A
        x2row = [AR_C[:, 0:8192].bitcast(F32), AR_C[:, 8192:16384].bitcast(F32)]
        ln2bf = [Vsb[:, 0:4096], Vsb[:, 4096:8192]]
        xe2 = [kT_f32[:, 1024:1536], kT_f32[:, 1536:2048]]
        yst = [ropeA[:, :], ropeB[:, :]]

        sem_names = ["qS", "qO", "qG", "cT", "cV", "cS"]
        sems = {nm: e(nc.semaphore(nm)) for nm in sem_names}
        for nm in sem_names:
            P.n[nm] = 0

        def dma(eng, out, in_, q):
            v = P.bump(q, 16)
            P.emit(eng, lambda E, o=out, i=in_, s=q: E.dma_start(out=o, in_=i)
                   .then_inc(E._sems[s], 16))
            return v

        def mm(out, lhsT, rhs, start, stop, inc=False):
            if inc:
                v = P.bump("cT", 1)
                P.emit("tensor", lambda E, o=out, l=lhsT, r=rhs, a=start, b=stop:
                       E.matmul(o, l, r, start=a, stop=b, skip_group_check=True)
                       .then_inc(E._sems["cT"], 1))
                return v
            P.emit("tensor", lambda E, o=out, l=lhsT, r=rhs, a=start, b=stop:
                   E.matmul(o, l, r, start=a, stop=b, skip_group_check=True))
            return None

        def mm_group(psum_ap, lhs_of, rhs_of, n=32):
            for d in range(n):
                mm(psum_ap, lhs_of(d), rhs_of(d), d == 0, d == n - 1, inc=(d == n - 1))
            return P.n["cT"]

        def v_op(fn, inc=False):
            if inc:
                v = P.bump("cV", 1)
                P.emit("vector", lambda E, f=fn: f(E).then_inc(E._sems["cV"], 1))
                return v
            P.emit("vector", lambda E, f=fn: f(E))
            return None

        def s_op(fn, inc=False):
            if inc:
                v = P.bump("cS", 1)
                P.emit("scalar", lambda E, f=fn: f(E).then_inc(E._sems["cS"], 1))
                return v
            P.emit("scalar", lambda E, f=fn: f(E))
            return None

        def t_op(fn, inc=False):
            if inc:
                v = P.bump("cT", 1)
                P.emit("tensor", lambda E, f=fn: f(E).then_inc(E._sems["cT"], 1))
                return v
            P.emit("tensor", lambda E, f=fn: f(E))
            return None

        S1, S2, MV, TM, VV, YY, T2 = range(7)

        def sA(arr, b0, nb):
            return stats[:, arr * 8 + b0: arr * 8 + b0 + nb]

        def sA0(arr, b):
            return stats[:, arr * 8 + b: arr * 8 + b + 1]

        def ln_chain(nb, spacer_ap_a, spacer_ap_b, iters=3):
            def sp():
                v_op(lambda E: E.tensor_add(spacer_ap_a, spacer_ap_b,
                                            spacer_ap_b))

            v_op(lambda E: E.tensor_scalar_mul(sA(MV, 0, nb), sA(S1, 0, nb),
                                               -1.0 / D))
            steps = [
                lambda E: E.tensor_mul(sA(TM, 0, nb), sA(MV, 0, nb),
                                       sA(MV, 0, nb)),
                lambda E: E.scalar_tensor_tensor(
                    sA(VV, 0, nb), sA(S2, 0, nb), 1.0 / D, sA(TM, 0, nb),
                    ALU.mult, ALU.subtract),
                lambda E: E.tensor_scalar_add(sA(VV, 0, nb), sA(VV, 0, nb), EPS),
                lambda E: E.tensor_scalar_add(sA(T2, 0, nb), sA(VV, 0, nb), 1.0),
                lambda E: E.reciprocal(sA(YY, 0, nb), sA(T2, 0, nb)),
                lambda E: E.tensor_scalar_mul(sA(YY, 0, nb), sA(YY, 0, nb), 2.0),
            ]
            for _ in range(iters):
                steps += [
                    lambda E: E.tensor_mul(sA(TM, 0, nb), sA(YY, 0, nb),
                                           sA(YY, 0, nb)),
                    lambda E: E.tensor_mul(sA(TM, 0, nb), sA(TM, 0, nb),
                                           sA(VV, 0, nb)),
                    lambda E: E.tensor_scalar_mul(sA(TM, 0, nb), sA(TM, 0, nb),
                                                  -0.5),
                    lambda E: E.tensor_scalar_add(sA(TM, 0, nb), sA(TM, 0, nb),
                                                  1.5),
                    lambda E: E.tensor_mul(sA(YY, 0, nb), sA(YY, 0, nb),
                                           sA(TM, 0, nb)),
                ]
            for fn in steps:
                sp()
                v_op(fn)
            return v_op(lambda E: E.tensor_add(spacer_ap_a, spacer_ap_b,
                                               spacer_ap_b), inc=True)

        dma("sync", mask_sb[:], maskt, "qS")
        dma("sync", id_sb[:], identt, "qS")
        qS_pre = dma("sync", p64_sb[:], p64t, "qS")
        v_op(lambda E: E.memset(ones_bf[:], 1.0))
        v_op(lambda E: E.memset(ones_f[:], 1.0))
        v_op(lambda E: E.memset(stats[:], 1.0), inc=True)
        wk_piece = []
        for a in range(4):
            wk_piece.append(dma(
                "gpsimd",
                wk_bf.rearrange("p (a c) -> p a c", c=256)[:, 8 * a:8 * a + 8],
                wkt.rearrange("(a p) c -> p a c", p=128)[:, 8 * a:8 * a + 8],
                "qG"))
        wv_piece = []
        for a in range(4):
            wv_piece.append(dma(
                "gpsimd",
                wv_bf.rearrange("p (a c) -> p a c", c=256)[:, 8 * a:8 * a + 8],
                wvt.rearrange("(a p) c -> p a c", p=128)[:, 8 * a:8 * a + 8],
                "qG"))
        qG_wkv = P.n["qG"]

        snap_norm = {}
        snap_red = {}
        snap_tile_tr = {}
        snap_trgrp = {}
        snap_trev = {}
        snap_lnTmm = {}
        snap_spill = {}
        snap_lnTev = {}
        snap_rope = {}
        snap_vcp = {}
        snap_cos = {}
        snap_ropedone = {}

        gi_ctr = [0]
        ab_ctr = [0]
        TRB = [pbG, pbH]
        PAB = [pbA[:, :], pbB[:, :]]
        PEF = [pbE[:, :], pbF[:, :]]
        PCD = [pbC[:, :], pbD[:, :]]

        def transpose_group(src_of, dest_ap, n=8):
            gi = gi_ctr[0]
            bank = TRB[gi % 2]
            if gi - 2 in snap_trev:
                P.wait("tensor", "cV", snap_trev[gi - 2])
            for j in range(n):
                if j == n - 1:
                    snap_trgrp[gi] = P.bump("cT", 1)
                    P.emit("tensor", lambda E, o=bank[:, j * 128:(j + 1) * 128],
                           i=src_of(j): E.transpose(o, i, id_sb[:])
                           .then_inc(E._sems["cT"], 1))
                else:
                    P.emit("tensor", lambda E, o=bank[:, j * 128:(j + 1) * 128],
                           i=src_of(j): E.transpose(o, i, id_sb[:]))
            P.wait("vector", "cT", snap_trgrp[gi])
            src3 = bank[:, 0:n * 128].rearrange("p (a t) -> p a t", t=128)
            snap_trev[gi] = v_op(
                lambda E, d=dest_ap, s=src3: E.tensor_copy(out=d, in_=s), inc=True)
            gi_ctr[0] += 1
            return snap_trev[gi]

        def rope_emit(proj_psum, dest_ap):
            ab = ab_ctr[0]
            cp = v_op(lambda E, p=proj_psum: E.tensor_copy(out=qrot[:], in_=p),
                      inc=True)
            rb = PEF[ab % 2]
            P.wait("tensor", "cV", cp)
            if ab - 2 in snap_rope:
                P.wait("tensor", "cV", snap_rope[ab - 2])
            rt = t_op(lambda E, r=rb: E.matmul(
                r, p64_sb[:], qrot[:], start=True, stop=True,
                skip_group_check=True), inc=True)
            P.wait("vector", "cT", rt)
            v_op(lambda E, p=proj_psum: E.tensor_mul(ropeA[:], p, cos_sb[:]))
            v_op(lambda E, r=rb: E.tensor_mul(ropeB[:], r, sin_sb[:]))
            snap_rope[ab] = v_op(
                lambda E, d=dest_ap: E.tensor_add(d, ropeA[:], ropeB[:]), inc=True)
            ab_ctr[0] += 1
            return snap_rope[ab]

        for c in range(8):
            if c - 1 in snap_ropedone:
                P.wait("scalar", "cV", snap_ropedone[c - 1])
            dma("scalar", cos_sb[:], cost[:, 512 * c:512 * (c + 1)], "qO")
            snap_cos[c] = dma("scalar", sin_sb[:], sint[:, 512 * c:512 * (c + 1)], "qO")

            for st in range(4):
                ti = 4 * c + st
                if st >= 2:
                    P.wait("sync", "cV", snap_red[ti - 2])
                elif c >= 1:
                    P.wait("sync", "cV", snap_norm[4 * c - 2 + st])
                qs_x = dma("sync", xbuf[st % 2][:, :],
                           xb[128 * ti:128 * (ti + 1), :], "qS")
                P.wait("vector", "qS", qs_x)
                xt = xbuf[st % 2][:, :]
                v_op(lambda E, x=xt, o=sA0(S1, st): E.reduce_sum(o, x, axis=AX.X))
                snap_red[ti] = v_op(
                    lambda E, x=xt, o=sA0(S2, st): E.scalar_tensor_tensor(
                        junk[:], x, 1.0, x, ALU.mult, ALU.mult, accum_out=o),
                    inc=True)
            jt = junk.rearrange("p (a t) -> p a t", t=1024)
            ln_chain(4, jt[:, 2, :], jt[:, 3, :], iters=2)
            for st in range(4):
                ti = 4 * c + st
                if st >= 2:
                    P.wait("sync", "cV", snap_norm[ti - 2])
                else:
                    P.wait("sync", "cV", snap_red[4 * c + 2 + st])
                qs_x = dma("sync", xbuf[st % 2][:, :],
                           xb[128 * ti:128 * (ti + 1), :], "qS")
                P.wait("vector", "qS", qs_x)
                xt = xbuf[st % 2][:, :]
                if ti - 2 in snap_tile_tr:
                    P.wait("vector", "cT", snap_tile_tr[ti - 2])
                snap_norm[ti] = v_op(
                    lambda E, x=xt, o=lnbf[st % 2][:, :], s_=st:
                    E.tensor_scalar(o, x, sA0(MV, s_), sA0(YY, s_),
                                    ALU.add, ALU.mult), inc=True)
                P.wait("tensor", "cV", snap_norm[ti])
                if ti == 0:
                    P.wait("tensor", "qS", qS_pre)
                for grp in range(4):
                    if st == 0 and grp == 0:
                        if c - 2 in snap_lnTmm:
                            P.wait("vector", "cT", snap_lnTmm[c - 2])
                            P.wait("vector", "qO", snap_spill[c - 2])
                    src = lnbf[st % 2]
                    dst = lnT[c % 2].rearrange("p (a t) -> p a t", t=512)[
                        :, 8 * grp:8 * grp + 8, st * 128:(st + 1) * 128]
                    transpose_group(
                        lambda j, s=src, g=grp:
                        s[:, (8 * g + j) * 128:(8 * g + j + 1) * 128], dst)
                snap_tile_tr[ti] = P.n["cT"]
            snap_lnTev[c] = P.n["cV"]

            lnc = lnT[c % 2].rearrange("p (a t) -> p a t", t=512)
            for kk in range(NKV):
                bank = PAB[ab_ctr[0] % 2]
                if ab_ctr[0] - 2 in snap_rope:
                    P.wait("tensor", "cV", snap_rope[ab_ctr[0] - 2])
                P.wait("tensor", "cV", snap_lnTev[c])
                if c == 0 and kk == 0:
                    for d in range(32):
                        if d % 8 == 0:
                            P.wait("tensor", "qG", wk_piece[d // 8])
                        mm(bank,
                           wk_bf[:, d * 256 + kk * 128: d * 256 + (kk + 1) * 128],
                           lnc[:, d, :], d == 0, d == 31, inc=(d == 31))
                    kmm = P.n["cT"]
                else:
                    kmm = mm_group(
                        bank,
                        lambda d, k=kk: wk_bf[:, d * 256 + k * 128: d * 256 + (k + 1) * 128],
                        lambda d: lnc[:, d, :])
                P.wait("vector", "cT", kmm)
                P.wait("vector", "qO", snap_cos[c])
                rope_emit(bank, kT[:, kk * T + 512 * c: kk * T + 512 * (c + 1)])
            snap_ropedone[c] = P.n["cV"]
            for vv in range(NKV):
                bank = PCD[vv]
                if (c - 1, vv) in snap_vcp:
                    P.wait("tensor", "cV", snap_vcp[(c - 1, vv)])
                if c == 0 and vv == 0:
                    for d in range(32):
                        if d % 8 == 0:
                            P.wait("tensor", "qG", wv_piece[d // 8])
                        mm(bank,
                           wv_bf[:, d * 256 + vv * 128: d * 256 + (vv + 1) * 128],
                           lnc[:, d, :], d == 0, d == 31, inc=(d == 31))
                    vmm = P.n["cT"]
                else:
                    vmm = mm_group(
                        bank,
                        lambda d, k=vv: wv_bf[:, d * 256 + k * 128: d * 256 + (k + 1) * 128],
                        lambda d: lnc[:, d, :])
                P.wait("vector", "cT", vmm)
                snap_vcp[(c, vv)] = v_op(
                    lambda E, b=bank, o=vtst[:, vv * 512:(vv + 1) * 512]:
                    E.tensor_copy(out=o, in_=b), inc=True)
            P.wait("tensor", "cV", snap_vcp[(c, 1)])
            transpose_group(
                lambda j: vtst[:, (j % 2) * 512 + (j // 2) * 128:
                               (j % 2) * 512 + (j // 2) * 128 + 128],
                Vsb[:, c * 1024:(c + 1) * 1024].rearrange(
                    "p (a t) -> p a t", t=128))
            snap_lnTmm[c] = P.n["cT"]
            P.wait("scalar", "cV", snap_lnTev[c])
            for a in range(4):
                dma("scalar",
                    lnT_sc.rearrange("(a p) t -> p a t", p=128)[
                        :, 8 * a:8 * a + 8, 512 * c:512 * (c + 1)],
                    lnT[c % 2].rearrange("p (a t) -> p a t", t=512)[
                        :, 8 * a:8 * a + 8, :], "qO")
            snap_spill[c] = P.n["qO"]

        cT_P1 = P.n["cT"]
        cV_P1 = P.n["cV"]
        cS_P1 = P.n["cS"]

        snap_p2mm = {}
        for qp in range(2):
            if qp == 0:
                P.wait("gpsimd", "cT", cT_P1)
            else:
                P.wait("gpsimd", "cT", snap_p2mm[(0, 7)])
            qg_piece = []
            for a in range(4):
                qg_piece.append(dma(
                    "gpsimd",
                    wq_bf.rearrange("p (a c) -> p a c", c=512)[:, 8 * a:8 * a + 8],
                    wqt.rearrange("(a p) c -> p a c", p=128)[
                        :, 8 * a:8 * a + 8, 512 * qp:512 * (qp + 1)], "qG"))
            qG_wq = P.n["qG"]
            for c in range(8):
                if qp == 0 and c < 2:
                    P.wait("sync", "cT", cT_P1)
                    P.wait("sync", "qO", snap_spill[7])
                prev = (qp, c - 2) if c >= 2 else ((qp - 1, 6 + c) if qp else None)
                if prev in snap_p2mm:
                    P.wait("sync", "cT", snap_p2mm[prev])
                for a in range(4):
                    dma("sync",
                        lnT[c % 2].rearrange("p (a t) -> p a t", t=512)[
                            :, 8 * a:8 * a + 8, :],
                        lnT_sc.rearrange("(a p) t -> p a t", p=128)[
                            :, 8 * a:8 * a + 8, 512 * c:512 * (c + 1)], "qS")
                qs_ln = P.n["qS"]
                key_prev = ("p2", qp, c - 1) if c else (
                    ("p2", 0, 7) if qp else 7)
                if key_prev in snap_ropedone:
                    P.wait("scalar", "cV", snap_ropedone[key_prev])
                dma("scalar", cos_sb[:], cost[:, 512 * c:512 * (c + 1)], "qO")
                sc_cos = dma("scalar", sin_sb[:], sint[:, 512 * c:512 * (c + 1)], "qO")
                lnc = lnT[c % 2].rearrange("p (a t) -> p a t", t=512)
                for hh in range(4):
                    H = 4 * qp + hh
                    bank = PAB[ab_ctr[0] % 2]
                    if ab_ctr[0] - 2 in snap_rope:
                        P.wait("tensor", "cV", snap_rope[ab_ctr[0] - 2])
                    P.wait("tensor", "qS", qs_ln)
                    if c == 0 and hh == 0:
                        for d in range(32):
                            if d % 8 == 0:
                                P.wait("tensor", "qG", qg_piece[d // 8])
                            mm(bank,
                               wq_bf[:, d * 512 + hh * 128:
                                     d * 512 + (hh + 1) * 128],
                               lnc[:, d, :], d == 0, d == 31, inc=(d == 31))
                        qmm = P.n["cT"]
                    else:
                        if hh == 0 and c == 0:
                            P.wait("tensor", "qG", qG_wq)
                        qmm = mm_group(
                            bank,
                            lambda d, h=hh: wq_bf[:, d * 512 + h * 128:
                                                  d * 512 + (h + 1) * 128],
                            lambda d: lnc[:, d, :])
                    P.wait("vector", "cT", qmm)
                    P.wait("vector", "qO", sc_cos)
                    if qp == 0 and c == 0 and hh == 0:
                        P.wait("vector", "cT", cT_P1)
                    rope_emit(bank, qT[:, H * T + 512 * c: H * T + 512 * (c + 1)])
                snap_ropedone[("p2", qp, c)] = P.n["cV"]
                snap_p2mm[(qp, c)] = P.n["cT"]

        cT_P2 = P.n["cT"]
        cV_P2 = P.n["cV"]

        if dbg:
            P.wait("sync", "cV", cV_P2)
            dma("sync", d_kT, kT[:, :], "qS")
            dma("sync", d_V, Vsb[:, :], "qS")
            dma("sync", d_qT, qT[:, :], "qS")

        snap_exp = {}
        snap_mask = {}
        snap_ctxmm = {}
        snap_iter = {}
        snap_ctxdma = {}
        kglob = 0
        it_ctr = 0
        h_end_cT = {}
        ag_qO = {}
        RB = pbG[:, :].bitcast(F32)

        SC3 = [pbA[:, :], pbB[:, :], pbH[:, :].bitcast(F32)]
        pending_outer = [None]
        for h in range(NKV):
            for s in range(8):
                for g in range(4):
                    H = 4 * h + g
                    nk = 4 * s + 4
                    su = PEF[it_ctr % 2]
                    cx = PCD[it_ctr % 2]
                    if it_ctr - 2 in snap_iter:
                        P.wait("tensor", "cV", snap_iter[it_ctr - 2])
                    for k in range(nk):
                        sb_ = SC3[kglob % 3]
                        eb = EB[kglob % 4]
                        if kglob - 3 in snap_exp:
                            P.wait("tensor", "cS", snap_exp[kglob - 3])
                        if kglob == 0:
                            P.wait("tensor", "cV", cV_P2)
                        smm = mm(sb_,
                                 kT[:, h * T + 128 * k: h * T + 128 * (k + 1)],
                                 qT[:, H * T + 512 * s: H * T + 512 * (s + 1)],
                                 True, True, inc=True)
                        if k == 0 and pending_outer[0] is not None:
                            pending_outer[0]()
                            pending_outer[0] = None
                        if kglob - 4 in snap_ctxmm:
                            P.wait("scalar", "cT", snap_ctxmm[kglob - 4])
                        P.wait("scalar", "cT", smm)
                        if kglob == 0:
                            P.wait("scalar", "cV", cV_P2)
                        snap_exp[kglob] = s_op(
                            lambda E, p=sb_, o=eb: E.activation(
                                o, p, AF.Exp, scale=ISQ), inc=True)
                        if k >= 4 * s:
                            off = 384 - (128 * k - 512 * s)
                            P.wait("vector", "cS", snap_exp[kglob])
                            snap_mask[kglob] = v_op(
                                lambda E, o=eb, m=mask_sb[:, off:off + 512]:
                                E.tensor_tensor(out=o, in0=o, in1=m, op=ALU.mult),
                                inc=True)
                            P.wait("tensor", "cV", snap_mask[kglob])
                        else:
                            P.wait("tensor", "cS", snap_exp[kglob])
                        mm(su, ones_bf[:, :], eb, k == 0, k == nk - 1)
                        snap_ctxmm[kglob] = mm(
                            cx, Vsb[:, k * 256 + h * 128: k * 256 + (h + 1) * 128],
                            eb, k == 0, k == nk - 1, inc=True)
                        kglob += 1
                    P.wait("vector", "cT", snap_ctxmm[kglob - 1])
                    rec = ropeB[0:1, :]
                    vrec = v_op(lambda E, s_=su: E.reciprocal(rec, s_[0:1, :]),
                                inc=True)
                    if it_ctr == 63:
                        P.wait("tensor", "cV", vrec)
                        romm = t_op(lambda E: E.matmul(
                            RB, ones_f[:], rec, start=True, stop=True,
                            skip_group_check=True), inc=True)
                    else:
                        romm = P.n["cT"] + 2

                        def do_outer(vr=vrec, expect=romm):
                            P.wait("tensor", "cV", vr)
                            got = t_op(lambda E: E.matmul(
                                RB, ones_f[:], rec, start=True, stop=True,
                                skip_group_check=True), inc=True)
                            assert got == expect, (got, expect)
                        pending_outer[0] = do_outer
                    P.wait("vector", "cT", romm)
                    v_op(lambda E: E.tensor_copy(out=ropeA[:], in_=RB))
                    if it_ctr - 2 in snap_ctxdma:
                        P.wait("vector", "qS", snap_ctxdma[it_ctr - 2])
                    cst = ctxst[it_ctr % 2]
                    vdone = v_op(lambda E, c_=cx, o=cst: E.tensor_tensor(
                        out=o, in0=c_, in1=ropeA[:], op=ALU.mult), inc=True)
                    snap_iter[it_ctr] = vdone
                    P.wait("sync", "cV", vdone)
                    snap_ctxdma[it_ctr] = dma(
                        "sync",
                        cc_in[h][s // 2][128 * g:128 * (g + 1),
                                         512 * (s % 2):512 * (s % 2 + 1)],
                        cst, "qS")
                    it_ctr += 1
                if s % 2 == 1:
                    ag_qO[(h, s // 2)] = P.n["qS"]
            h_end_cT[h] = P.n["cT"]
        assert pending_outer[0] is None

        def emit_ag(h, qt):
            P.wait("gpsimd", "qS", ag_qO[(h, qt)])
            v = P.bump("qG", 1)
            P.emit("gpsimd", lambda E, hh=h, q=qt: E.collective_compute(
                "AllGather", ALU.bypass,
                replica_groups=[[0, 1, 2, 3], [4, 5, 6, 7]],
                ins=[cc_in[hh][q]],
                outs=[cc_out[hh][q]]).then_inc(E._sems["qG"], 1))
            return v

        def emit_switch_loads(h):
            for _ in range(16):
                P.bump("qG", 16)

            def do_switch(E, hh=h):
                pid = E.partition_id()
                qv = E.compute_val(pid % 4)
                for body in nc.Switch(engines=[E], index=[qv], n=4):
                    for blk in range(16):
                        E.dma_start(
                            out=ctxF[:, (hh * 16 + blk) * 1024:
                                     (hh * 16 + blk + 1) * 1024],
                            in_=cc_out[hh][body][blk * 128:(blk + 1) * 128, :],
                        ).then_inc(E._sems["qG"], 16)
            P.emit("gpsimd", do_switch)
            return P.n["qG"]

        def emit_wo_cast(e_, wait_cT):
            P.wait("gpsimd", "cT", wait_cT)
            for a in range(4):
                dma("gpsimd",
                    wo_bf[e_ % 2].rearrange("p (a c) -> p a c", c=512)[
                        :, 8 * a:8 * a + 8],
                    wot.rearrange("(a p) c -> p a c", p=128)[
                        :, 8 * a:8 * a + 8, 512 * e_:512 * (e_ + 1)], "qG")
            return P.n["qG"]

        for qt in range(4):
            emit_ag(0, qt)
        ag0_last = P.n["qG"]
        P.wait("gpsimd", "cT", cT_P2)
        P.wait("gpsimd", "qG", ag0_last)
        qG_ld0 = emit_switch_loads(0)
        emit_ag(1, 0)
        emit_ag(1, 1)
        snap_wocast = {0: emit_wo_cast(0, h_end_cT[0])}
        emit_ag(1, 2)
        emit_ag(1, 3)
        P.wait("gpsimd", "qG", P.n["qG"])
        qG_ld1 = emit_switch_loads(1)
        snap_wocast[1] = emit_wo_cast(1, h_end_cT[1])

        snap_p5mm = {}
        snap_evac5 = {}
        snap_x2dma = {}
        ctxFv = ctxF.rearrange("p (a t) -> p a t", t=1024)

        if dbg:
            P.wait("sync", "qG", qG_ld1)
            dma("sync", d_ctxF, ctxF[:, :], "qS")
            P.wait("sync", "qG", snap_wocast[0])
            dma("sync", d_wo0, wo_bf[0], "qS")

        xeC = kT_f32[:, :].rearrange("p (a t) -> p a t", t=512)
        snap_xeC = {}
        tt_ctr = 0
        for e_ in range(8):
            wob = wo_bf[e_ % 2].rearrange("p (a c) -> p a c", c=512)
            if e_ == 0:
                P.wait("sync", "cT", h_end_cT[1])
            else:
                P.wait("sync", "cV", snap_evac5[8 * e_ - 1])
            snap_xeC[e_] = dma(
                "sync", xeC,
                xq.rearrange("(a p) d -> p a d", p=128)[:, :,
                                                        512 * e_:512 * (e_ + 1)],
                "qS")
            for tt in range(8):
                bank = PAB[tt_ctr % 2]
                if tt_ctr - 2 in snap_evac5:
                    P.wait("tensor", "cV", snap_evac5[tt_ctr - 2])
                if tt == 0:
                    P.wait("tensor", "qG", snap_wocast[e_])
                    if e_ == 0:
                        P.wait("tensor", "qG", qG_ld1)
                xet = xeC[:, tt, :]
                qs_xe = snap_xeC[e_]
                pm = mm_group(bank,
                              lambda d, t_=tt: ctxFv[:, d, t_ * 128:(t_ + 1) * 128],
                              lambda d: wob[:, d, :])
                snap_p5mm[(e_, tt)] = pm
                if dbg and e_ == 0 and tt == 0:
                    P.wait("vector", "cT", pm)
                    vm5 = v_op(lambda E, p=bank: E.tensor_copy(
                        out=ropeA[:], in_=p), inc=True)
                    P.wait("scalar", "cV", vm5)
                    qp5 = dma("scalar", d_op, ropeA[:], "qO")
                    P.wait("vector", "qO", qp5)
                P.wait("vector", "cT", pm)
                P.wait("vector", "qS", qs_xe)
                if tt_ctr - 2 in snap_x2dma:
                    P.wait("vector", "qO", snap_x2dma[tt_ctr - 2])
                x2s = x2st[tt_ctr % 2]
                c1 = p5st[:, tt * 16 + e_: tt * 16 + e_ + 1]
                c2 = p5st[:, tt * 16 + 8 + e_: tt * 16 + 8 + e_ + 1]
                v_op(lambda E, b=bank, x=xet, o=x2s, a=c1: E.scalar_tensor_tensor(
                    o, b, 1.0, x, ALU.mult, ALU.add, accum_out=a))
                j5 = Vsb[:, 2048:2560]
                snap_evac5[tt_ctr] = v_op(
                    lambda E, o=j5, x=x2s, a=c2: E.scalar_tensor_tensor(
                        o, x, 1.0, x, ALU.mult, ALU.mult, accum_out=a), inc=True)
                P.wait("scalar", "cV", snap_evac5[tt_ctr])
                snap_x2dma[tt_ctr] = dma(
                    "scalar",
                    x2_sc[128 * tt:128 * (tt + 1), 512 * e_:512 * (e_ + 1)],
                    x2s, "qO")
                tt_ctr += 1
            if e_ + 2 <= 7:
                snap_wocast[e_ + 2] = emit_wo_cast(e_ + 2, snap_p5mm[(e_, 7)])

        cV_P5 = P.n["cV"]
        qO_P5 = P.n["qO"]

        snap_n2 = {}
        snap_tr2t = {}
        for tt in range(8):
            v_op(lambda E, o=sA0(S1, tt), i=p5st[:, tt * 16:tt * 16 + 8]:
                 E.reduce_sum(o, i, axis=AX.X))
            v_op(lambda E, o=sA0(S2, tt), i=p5st[:, tt * 16 + 8:tt * 16 + 16]:
                 E.reduce_sum(o, i, axis=AX.X))
        ln_chain(8, kT_f32[:, 2048:2560], kT_f32[:, 2560:3072], iters=5)
        for tt in range(8):
            par = tt % 2
            if tt - 2 in snap_n2:
                P.wait("sync", "cV", snap_n2[tt - 2])
            if tt == 0:
                P.wait("sync", "qO", qO_P5)
            qs_x2 = dma("sync", x2row[par][:, :],
                        x2_sc[128 * tt:128 * (tt + 1), :], "qS")
            P.wait("vector", "qS", qs_x2)
            if tt - 2 in snap_tr2t:
                P.wait("vector", "cT", snap_tr2t[tt - 2])
            snap_n2[tt] = v_op(
                lambda E, x=x2row[par][:, :], o=ln2bf[par][:, :], t_=tt:
                E.tensor_scalar(o, x, sA0(MV, t_), sA0(YY, t_),
                                ALU.add, ALU.mult), inc=True)
            P.wait("tensor", "cV", snap_n2[tt])
            if tt == 0:
                P.wait("tensor", "cV", cV_P5)
            for grp in range(4):
                src = ln2bf[par]
                dst = ln2T.rearrange("p (a t) -> p a t", t=1024)[
                    :, 8 * grp:8 * grp + 8, tt * 128:(tt + 1) * 128]
                transpose_group(
                    lambda j, s=src, g=grp:
                    s[:, (8 * g + j) * 128:(8 * g + j + 1) * 128], dst)
            snap_tr2t[tt] = P.n["cT"]
        cV_ln2 = P.n["cV"]

        snap_itcast = {}
        snap_f6mm = {}
        snap_yev = {}
        snap_ydma = {}

        def emit_it_cast(e_, wait_cT):
            P.wait("gpsimd", "cT", wait_cT)
            for a in range(4):
                dma("gpsimd",
                    it_bf[e_ % 2].rearrange("p (a c) -> p a c", c=512)[
                        :, 8 * a:8 * a + 8],
                    itt.rearrange("(a p) c -> p a c", p=128)[
                        :, 8 * a:8 * a + 8, 512 * e_:512 * (e_ + 1)], "qG")
            return P.n["qG"]

        snap_itcast[0] = emit_it_cast(0, snap_p5mm[(6, 7)])
        snap_itcast[1] = emit_it_cast(1, snap_p5mm[(7, 7)])

        ln2Tv = ln2T.rearrange("p (a t) -> p a t", t=1024)
        tt6 = 0
        for e_ in range(8):
            itb = it_bf[e_ % 2].rearrange("p (a c) -> p a c", c=512)
            for tt in range(8):
                bank = PAB[tt6 % 2]
                if tt6 - 2 in snap_yev:
                    P.wait("tensor", "cV", snap_yev[tt6 - 2])
                if tt == 0:
                    P.wait("tensor", "qG", snap_itcast[e_])
                    if e_ == 0:
                        P.wait("tensor", "cV", cV_ln2)
                xet2 = xe2[tt6 % 2]
                if tt6 - 2 in snap_yev:
                    P.wait("sync", "cV", snap_yev[tt6 - 2])
                qs_xe2 = dma(
                    "sync", xet2,
                    x2_sc[128 * tt:128 * (tt + 1), 512 * e_:512 * (e_ + 1)], "qS")
                fm = mm_group(bank,
                              lambda d, t_=tt: ln2Tv[:, d, t_ * 128:(t_ + 1) * 128],
                              lambda d: itb[:, d, :])
                snap_f6mm[(e_, tt)] = fm
                P.wait("vector", "cT", fm)
                P.wait("vector", "qS", qs_xe2)
                if tt6 - 2 in snap_ydma:
                    P.wait("vector", "qO", snap_ydma[tt6 - 2])
                yb = yst[tt6 % 2]
                snap_yev[tt6] = v_op(
                    lambda E, b=bank, x=xet2, o=yb: E.tensor_add(o, b, x), inc=True)
                P.wait("scalar", "cV", snap_yev[tt6])
                snap_ydma[tt6] = dma(
                    "scalar", y[128 * tt:128 * (tt + 1), 512 * e_:512 * (e_ + 1)],
                    yb, "qO")
                tt6 += 1
            if e_ + 2 <= 7:
                snap_itcast[e_ + 2] = emit_it_cast(e_ + 2, snap_f6mm[(e_, 7)])

        if dbg:
            P.wait("sync", "qG", P.n["qG"])
            P.wait("sync", "qO", P.n["qO"])
            for h in range(NKV):
                for qt in range(4):
                    dma("sync", d_cc[h][:, 1024 * qt:1024 * (qt + 1)],
                        cc_out[h][qt], "qS")
            dma("sync", d_lnT, lnT_sc, "qS")
            dma("sync", d_x2, x2_sc, "qS")
            dma("sync", d_acc[:, 0:16], p5st[:, 0:16], "qS")

        P.wait("sync", "qO", P.n["qO"])
        P.wait("sync", "qG", P.n["qG"])
        P.wait("sync", "qS", P.n["qS"])

        with nc.Block() as block:

            def run(E, eng):
                E._sems = sems
                for fn in P.ops[eng]:
                    fn(E)

            @block.sync
            def _(E):
                run(E, "sync")

            @block.gpsimd
            def _(E):
                run(E, "gpsimd")

            @block.scalar
            def _(E):
                run(E, "scalar")

            @block.vector
            def _(E):
                run(E, "vector")

            @block.tensor
            def _(E):
                run(E, "tensor")

    return nc


def _rope_tables():
    ROPE_THETA = 500000.0
    ROPE_FACTOR = 32.0
    HI, LO = 4.0, 1.0
    ORIG = 8192
    d = np.arange(0, DH, 2, dtype=np.float64) / DH
    inv = 1.0 / ROPE_THETA ** d
    wav = 2.0 * math.pi / inv
    cut_lo = ORIG / HI
    cut_hi = ORIG / LO
    inv = np.where(wav > cut_hi, inv / ROPE_FACTOR, inv)
    smooth = (ORIG / wav - LO) / (HI - LO)
    smoothed = (1.0 - smooth) * inv / ROPE_FACTOR + smooth * inv
    med = (~(wav < cut_hi)) & (~(wav > cut_lo))
    inv = np.where(med, smoothed, inv)
    pos = np.arange(T, dtype=np.float64)
    freqs = pos[:, None] * inv[None, :]
    emb = np.concatenate([freqs, freqs], axis=1)
    cos = np.ascontiguousarray(np.cos(emb).T.astype(np.float32))
    sin = np.sin(emb).T.astype(np.float32)
    sin_signed = sin.copy()
    sin_signed[0:64] = -sin[0:64]
    return cos, np.ascontiguousarray(sin_signed)


def kernel(L0_attn_x, L0_attn_Wq, L0_attn_Wk, L0_attn_Wv, L0_attn_Wo,
           L0_ffn_identity):
    x = np.asarray(L0_attn_x, dtype=np.float32)
    Wq = np.asarray(L0_attn_Wq, dtype=np.float32)
    Wk = np.asarray(L0_attn_Wk, dtype=np.float32)
    Wv = np.asarray(L0_attn_Wv, dtype=np.float32)
    Wo = np.asarray(L0_attn_Wo, dtype=np.float32)
    I = np.asarray(L0_ffn_identity, dtype=np.float32)

    cosT, sinT = _rope_tables()
    mask = np.zeros((128, 896), dtype=ml_dtypes.bfloat16)
    for p in range(128):
        mask[p, p + 384:] = 1.0
    ident = np.eye(128, dtype=ml_dtypes.bfloat16)
    p64 = np.zeros((128, 128), dtype=ml_dtypes.bfloat16)
    for i in range(128):
        p64[i, i ^ 64] = 1.0

    perm = []
    for h in range(2):
        for r in range(4):
            for g in range(4):
                head = 8 * r + 4 * h + g
                perm.extend(range(head * 128, head * 128 + 128))
    WoT_perm = np.ascontiguousarray(Wo.T[np.array(perm)])
    IT = np.ascontiguousarray(I.T)

    in_maps = []
    for c in range(8):
        b, qq = c // 4, c % 4
        in_maps.append({
            "xb": np.ascontiguousarray(x[b]),
            "xq": np.ascontiguousarray(x[b, 1024 * qq:1024 * (qq + 1)]),
            "wqt": np.ascontiguousarray(Wq[1024 * qq:1024 * (qq + 1), :].T),
            "wkt": np.ascontiguousarray(Wk[256 * qq:256 * (qq + 1), :].T),
            "wvt": np.ascontiguousarray(Wv[256 * qq:256 * (qq + 1), :].T),
            "wot": WoT_perm,
            "itt": IT,
            "cost": cosT,
            "sint": sinT,
            "maskt": mask,
            "identt": ident,
            "p64t": p64,
        })

    nc = build_nc()
    res = run_bass_kernel_spmd(nc, in_maps, core_ids=list(range(8)))
    out = np.empty((2, T, D), dtype=np.float32)
    for c in range(8):
        b, qq = c // 4, c % 4
        out[b, 1024 * qq:1024 * (qq + 1)] = res.results[c]["y"]
    return out


# version 11 (final)
# speedup vs baseline: 1.5944x; 1.4594x over previous
import math
from contextlib import ExitStack

import numpy as np
import ml_dtypes

import concourse.bass as bass
import concourse.mybir as mybir
from concourse.bass_utils import run_bass_kernel_spmd

F32 = mybir.dt.float32
BF16 = mybir.dt.bfloat16
AF = mybir.ActivationFunctionType
ALU = mybir.AluOpType
AX = mybir.AxisListType

T = 4096
D = 4096
DH = 128
NH = 8
NKV = 2
QR = 1024
EPS = 1e-5
ISQ = 1.0 / math.sqrt(128.0)

ENGS = ("sync", "gpsimd", "scalar", "vector", "tensor")


class Prog:
    def __init__(self):
        self.ops = {e: [] for e in ENGS}
        self.n = {}

    def emit(self, eng, fn):
        self.ops[eng].append(fn)

    def wait(self, eng, sem, val):
        if val and val > 0:
            self.emit(eng, lambda E, s=sem, v=val: E.wait_ge(E._sems[s], v))

    def bump(self, sem, amt):
        self.n[sem] = self.n.get(sem, 0) + amt
        return self.n[sem]


def build_nc(dbg=False):
    nc = bass.Bass("TRN2", target_bir_lowering=False)

    xb = nc.dram_tensor("xb", [T, D], F32, kind="ExternalInput").ap()
    xq = nc.dram_tensor("xq", [QR, D], F32, kind="ExternalInput").ap()
    wqt = nc.dram_tensor("wqt", [D, NH * DH], F32, kind="ExternalInput").ap()
    wkt = nc.dram_tensor("wkt", [D, NKV * DH], F32, kind="ExternalInput").ap()
    wvt = nc.dram_tensor("wvt", [D, NKV * DH], F32, kind="ExternalInput").ap()
    wot = nc.dram_tensor("wot", [D, D], F32, kind="ExternalInput").ap()
    itt = nc.dram_tensor("itt", [D, D], F32, kind="ExternalInput").ap()
    cost = nc.dram_tensor("cost", [DH, T], F32, kind="ExternalInput").ap()
    sint = nc.dram_tensor("sint", [DH, T], F32, kind="ExternalInput").ap()
    maskt = nc.dram_tensor("maskt", [128, 896], BF16, kind="ExternalInput").ap()
    identt = nc.dram_tensor("identt", [128, 128], BF16, kind="ExternalInput").ap()
    p64t = nc.dram_tensor("p64t", [128, 128], BF16, kind="ExternalInput").ap()
    y = nc.dram_tensor("y", [QR, D], F32, kind="ExternalOutput").ap()

    lnT_sc = nc.dram_tensor("lnT_sc", [D, T], BF16, kind="Internal").ap()
    x2_sc = nc.dram_tensor("x2_sc", [QR, D], F32, kind="Internal").ap()
    cc_in = [[nc.dram_tensor(f"cc_in{h}_{qt}", [4 * DH, 1024], BF16,
                             kind="Internal").ap() for qt in range(4)]
             for h in range(NKV)]
    cc_out = [[nc.dram_tensor(f"cc_out{h}_{qt}", [16 * DH, 1024], BF16,
                              kind="Internal").ap() for qt in range(4)]
              for h in range(NKV)]

    if dbg:
        d_lnT = nc.dram_tensor("d_lnT", [D, T], BF16, kind="ExternalOutput").ap()
        d_kT = nc.dram_tensor("d_kT", [128, 8192], BF16, kind="ExternalOutput").ap()
        d_V = nc.dram_tensor("d_V", [128, 8192], BF16, kind="ExternalOutput").ap()
        d_qT = nc.dram_tensor("d_qT", [128, 32768], BF16, kind="ExternalOutput").ap()
        d_cc = [nc.dram_tensor(f"d_cc{h}", [16 * DH, T], BF16, kind="ExternalOutput").ap()
                for h in range(NKV)]
        d_x2 = nc.dram_tensor("d_x2", [QR, D], F32, kind="ExternalOutput").ap()
        d_ctxF = nc.dram_tensor("d_ctxF", [128, 32768], BF16, kind="ExternalOutput").ap()
        d_op = nc.dram_tensor("d_op", [128, 512], F32, kind="ExternalOutput").ap()
        d_wo0 = nc.dram_tensor("d_wo0", [128, 16384], BF16, kind="ExternalOutput").ap()
        d_acc = nc.dram_tensor("d_acc", [128, 16], F32, kind="ExternalOutput").ap()

    P = Prog()

    with ExitStack() as ctx:
        e = ctx.enter_context

        AR_A = e(nc.sbuf_tensor("AR_A", [128, 32768], BF16))
        AR_B = e(nc.sbuf_tensor("AR_B", [128, 32768], BF16))
        AR_C = e(nc.sbuf_tensor("AR_C", [128, 16384], BF16))
        kT = e(nc.sbuf_tensor("kT", [128, NKV * T], BF16))
        Vsb = e(nc.sbuf_tensor("Vsb", [128, 8192], BF16))
        cos_sb = e(nc.sbuf_tensor("cos_sb", [128, 512], F32))
        sin_sb = e(nc.sbuf_tensor("sin_sb", [128, 512], F32))
        mask_sb = e(nc.sbuf_tensor("mask_sb", [128, 896], BF16))
        id_sb = e(nc.sbuf_tensor("id_sb", [128, 128], BF16))
        p64_sb = e(nc.sbuf_tensor("p64_sb", [128, 128], BF16))
        ones_bf = e(nc.sbuf_tensor("ones_bf", [128, 128], BF16))
        ones_f = e(nc.sbuf_tensor("ones_f", [1, 128], F32))
        ropeA = e(nc.sbuf_tensor("ropeA", [128, 512], F32))
        ropeB = e(nc.sbuf_tensor("ropeB", [128, 512], F32))
        qrot = e(nc.sbuf_tensor("qrot", [128, 512], BF16))
        vtst = e(nc.sbuf_tensor("vtst", [128, 1024], BF16))
        stats = e(nc.sbuf_tensor("stats", [128, 64], F32))
        p5st = e(nc.sbuf_tensor("p5st", [128, 128], F32))

        pbA = e(nc.psum_tensor("pbA", [128, 512], F32))
        pbB = e(nc.psum_tensor("pbB", [128, 512], F32))
        pbC = e(nc.psum_tensor("pbC", [128, 512], F32))
        pbD = e(nc.psum_tensor("pbD", [128, 512], F32))
        pbE = e(nc.psum_tensor("pbE", [128, 512], F32))
        pbF = e(nc.psum_tensor("pbF", [128, 512], F32))
        pbG = e(nc.psum_tensor("pbG", [128, 1024], BF16))
        pbH = e(nc.psum_tensor("pbH", [128, 1024], BF16))

        lnT = [AR_A[:, 0:16384], AR_A[:, 16384:32768]]
        xbuf = [AR_B[:, 0:8192].bitcast(F32), AR_B[:, 8192:16384].bitcast(F32)]
        lnbf = [AR_B[:, 16384:20480], AR_B[:, 20480:24576]]
        junk = AR_B[:, 24576:28672]
        wk_bf = AR_C[:, 0:8192]
        wv_bf = AR_C[:, 8192:16384]
        wq_bf = AR_C[:, 0:16384]
        qT = AR_B
        EB = [cos_sb[:, :].bitcast(BF16)[:, 0:512],
              cos_sb[:, :].bitcast(BF16)[:, 512:1024],
              sin_sb[:, :].bitcast(BF16)[:, 0:512],
              sin_sb[:, :].bitcast(BF16)[:, 512:1024]]
        ctxst = [vtst[:, 0:512], vtst[:, 512:1024]]
        ctxF = AR_A
        wo_bf = [AR_B[:, 0:16384], AR_B[:, 16384:32768]]
        it_bf = wo_bf
        kT_f32 = kT[:, :].bitcast(F32)
        xe_t = [kT_f32[:, 0:512], kT_f32[:, 512:1024]]
        x2st = [Vsb[:, 4096:5120].bitcast(F32)[:, 0:512],
                Vsb[:, 5120:6144].bitcast(F32)[:, 0:512]]
        j5 = sin_sb[:, :].bitcast(BF16)[:, 512:1024]
        ln2T = AR_A
        x2row = [AR_C[:, 0:8192].bitcast(F32), AR_C[:, 8192:16384].bitcast(F32)]
        ln2bf = [Vsb[:, 0:4096], Vsb[:, 4096:8192]]
        xe2 = [kT_f32[:, 1024:1536], kT_f32[:, 1536:2048]]
        yst = [ropeA[:, :], ropeB[:, :]]

        sem_names = ["qS", "qO", "qG", "cT", "cV", "cS"]
        sems = {nm: e(nc.semaphore(nm)) for nm in sem_names}
        for nm in sem_names:
            P.n[nm] = 0

        def dma(eng, out, in_, q):
            v = P.bump(q, 16)
            P.emit(eng, lambda E, o=out, i=in_, s=q: E.dma_start(out=o, in_=i)
                   .then_inc(E._sems[s], 16))
            return v

        def mm(out, lhsT, rhs, start, stop, inc=False):
            if inc:
                v = P.bump("cT", 1)
                P.emit("tensor", lambda E, o=out, l=lhsT, r=rhs, a=start, b=stop:
                       E.matmul(o, l, r, start=a, stop=b, skip_group_check=True)
                       .then_inc(E._sems["cT"], 1))
                return v
            P.emit("tensor", lambda E, o=out, l=lhsT, r=rhs, a=start, b=stop:
                   E.matmul(o, l, r, start=a, stop=b, skip_group_check=True))
            return None

        def mm_group(psum_ap, lhs_of, rhs_of, n=32):
            for d in range(n):
                mm(psum_ap, lhs_of(d), rhs_of(d), d == 0, d == n - 1, inc=(d == n - 1))
            return P.n["cT"]

        def v_op(fn, inc=False):
            if inc:
                v = P.bump("cV", 1)
                P.emit("vector", lambda E, f=fn: f(E).then_inc(E._sems["cV"], 1))
                return v
            P.emit("vector", lambda E, f=fn: f(E))
            return None

        def s_op(fn, inc=False):
            if inc:
                v = P.bump("cS", 1)
                P.emit("scalar", lambda E, f=fn: f(E).then_inc(E._sems["cS"], 1))
                return v
            P.emit("scalar", lambda E, f=fn: f(E))
            return None

        def t_op(fn, inc=False):
            if inc:
                v = P.bump("cT", 1)
                P.emit("tensor", lambda E, f=fn: f(E).then_inc(E._sems["cT"], 1))
                return v
            P.emit("tensor", lambda E, f=fn: f(E))
            return None

        S1, S2, MV, TM, VV, YY, T2 = range(7)

        def sA(arr, b0, nb):
            return stats[:, arr * 8 + b0: arr * 8 + b0 + nb]

        def sA0(arr, b):
            return stats[:, arr * 8 + b: arr * 8 + b + 1]

        def ln_chain(nb, spacer_ap_a, spacer_ap_b, iters=3):
            def sp():
                v_op(lambda E: E.tensor_add(spacer_ap_a, spacer_ap_b,
                                            spacer_ap_b))

            v_op(lambda E: E.tensor_scalar_mul(sA(MV, 0, nb), sA(S1, 0, nb),
                                               -1.0 / D))
            steps = [
                lambda E: E.tensor_mul(sA(TM, 0, nb), sA(MV, 0, nb),
                                       sA(MV, 0, nb)),
                lambda E: E.scalar_tensor_tensor(
                    sA(VV, 0, nb), sA(S2, 0, nb), 1.0 / D, sA(TM, 0, nb),
                    ALU.mult, ALU.subtract),
                lambda E: E.tensor_scalar_add(sA(VV, 0, nb), sA(VV, 0, nb), EPS),
                lambda E: E.tensor_scalar_add(sA(T2, 0, nb), sA(VV, 0, nb), 1.0),
                lambda E: E.reciprocal(sA(YY, 0, nb), sA(T2, 0, nb)),
                lambda E: E.tensor_scalar_mul(sA(YY, 0, nb), sA(YY, 0, nb), 2.0),
            ]
            for _ in range(iters):
                steps += [
                    lambda E: E.tensor_mul(sA(TM, 0, nb), sA(YY, 0, nb),
                                           sA(YY, 0, nb)),
                    lambda E: E.tensor_mul(sA(TM, 0, nb), sA(TM, 0, nb),
                                           sA(VV, 0, nb)),
                    lambda E: E.tensor_scalar_mul(sA(TM, 0, nb), sA(TM, 0, nb),
                                                  -0.5),
                    lambda E: E.tensor_scalar_add(sA(TM, 0, nb), sA(TM, 0, nb),
                                                  1.5),
                    lambda E: E.tensor_mul(sA(YY, 0, nb), sA(YY, 0, nb),
                                           sA(TM, 0, nb)),
                ]
            for fn in steps:
                sp()
                v_op(fn)
            return v_op(lambda E: E.tensor_add(spacer_ap_a, spacer_ap_b,
                                               spacer_ap_b), inc=True)

        dma("sync", mask_sb[:], maskt, "qS")
        dma("sync", id_sb[:], identt, "qS")
        qS_pre = dma("sync", p64_sb[:], p64t, "qS")
        v_op(lambda E: E.memset(ones_bf[:], 1.0))
        v_op(lambda E: E.memset(ones_f[:], 1.0))
        v_op(lambda E: E.memset(stats[:], 1.0), inc=True)
        wk_piece = []
        for a in range(4):
            wk_piece.append(dma(
                "gpsimd",
                wk_bf.rearrange("p (a c) -> p a c", c=256)[:, 8 * a:8 * a + 8],
                wkt.rearrange("(a p) c -> p a c", p=128)[:, 8 * a:8 * a + 8],
                "qG"))
        wv_piece = []
        for a in range(4):
            wv_piece.append(dma(
                "gpsimd",
                wv_bf.rearrange("p (a c) -> p a c", c=256)[:, 8 * a:8 * a + 8],
                wvt.rearrange("(a p) c -> p a c", p=128)[:, 8 * a:8 * a + 8],
                "qG"))
        qG_wkv = P.n["qG"]

        snap_norm = {}
        snap_red = {}
        snap_tile_tr = {}
        snap_trgrp = {}
        snap_trev = {}
        snap_lnTmm = {}
        snap_spill = {}
        snap_lnTev = {}
        snap_rope = {}
        snap_vcp = {}
        snap_cos = {}
        snap_ropedone = {}

        gi_ctr = [0]
        ab_ctr = [0]
        TRB = [pbG, pbH]
        PAB = [pbA[:, :], pbB[:, :]]
        PEF = [pbE[:, :], pbF[:, :]]
        PCD = [pbC[:, :], pbD[:, :]]

        def transpose_group(src_of, dest_ap, n=8):
            gi = gi_ctr[0]
            bank = TRB[gi % 2]
            if gi - 2 in snap_trev:
                P.wait("tensor", "cV", snap_trev[gi - 2])
            for j in range(n):
                if j == n - 1:
                    snap_trgrp[gi] = P.bump("cT", 1)
                    P.emit("tensor", lambda E, o=bank[:, j * 128:(j + 1) * 128],
                           i=src_of(j): E.transpose(o, i, id_sb[:])
                           .then_inc(E._sems["cT"], 1))
                else:
                    P.emit("tensor", lambda E, o=bank[:, j * 128:(j + 1) * 128],
                           i=src_of(j): E.transpose(o, i, id_sb[:]))
            P.wait("vector", "cT", snap_trgrp[gi])
            src3 = bank[:, 0:n * 128].rearrange("p (a t) -> p a t", t=128)
            snap_trev[gi] = v_op(
                lambda E, d=dest_ap, s=src3: E.tensor_copy(out=d, in_=s), inc=True)
            gi_ctr[0] += 1
            return snap_trev[gi]

        def rope_emit(proj_psum, dest_ap):
            ab = ab_ctr[0]
            cp = v_op(lambda E, p=proj_psum: E.tensor_copy(out=qrot[:], in_=p),
                      inc=True)
            rb = PEF[ab % 2]
            P.wait("tensor", "cV", cp)
            if ab - 2 in snap_rope:
                P.wait("tensor", "cV", snap_rope[ab - 2])
            rt = t_op(lambda E, r=rb: E.matmul(
                r, p64_sb[:], qrot[:], start=True, stop=True,
                skip_group_check=True), inc=True)
            P.wait("vector", "cT", rt)
            v_op(lambda E, p=proj_psum: E.tensor_mul(ropeA[:], p, cos_sb[:]))
            v_op(lambda E, r=rb: E.tensor_mul(ropeB[:], r, sin_sb[:]))
            snap_rope[ab] = v_op(
                lambda E, d=dest_ap: E.tensor_add(d, ropeA[:], ropeB[:]), inc=True)
            ab_ctr[0] += 1
            return snap_rope[ab]

        for c in range(8):
            if c - 1 in snap_ropedone:
                P.wait("scalar", "cV", snap_ropedone[c - 1])
            dma("scalar", cos_sb[:], cost[:, 512 * c:512 * (c + 1)], "qO")
            snap_cos[c] = dma("scalar", sin_sb[:], sint[:, 512 * c:512 * (c + 1)], "qO")

            for st in range(4):
                ti = 4 * c + st
                if st >= 2:
                    P.wait("sync", "cV", snap_red[ti - 2])
                elif c >= 1:
                    P.wait("sync", "cV", snap_norm[4 * c - 2 + st])
                qs_x = dma("sync", xbuf[st % 2][:, :],
                           xb[128 * ti:128 * (ti + 1), :], "qS")
                P.wait("vector", "qS", qs_x)
                xt = xbuf[st % 2][:, :]
                v_op(lambda E, x=xt, o=sA0(S1, st): E.reduce_sum(o, x, axis=AX.X))
                snap_red[ti] = v_op(
                    lambda E, x=xt, o=sA0(S2, st): E.scalar_tensor_tensor(
                        junk[:], x, 1.0, x, ALU.mult, ALU.mult, accum_out=o),
                    inc=True)
            jt = junk.rearrange("p (a t) -> p a t", t=1024)
            ln_chain(4, jt[:, 2, :], jt[:, 3, :], iters=2)
            for st in range(4):
                ti = 4 * c + st
                if st >= 2:
                    P.wait("sync", "cV", snap_norm[ti - 2])
                else:
                    P.wait("sync", "cV", snap_red[4 * c + 2 + st])
                qs_x = dma("sync", xbuf[st % 2][:, :],
                           xb[128 * ti:128 * (ti + 1), :], "qS")
                P.wait("vector", "qS", qs_x)
                xt = xbuf[st % 2][:, :]
                if ti - 2 in snap_tile_tr:
                    P.wait("vector", "cT", snap_tile_tr[ti - 2])
                snap_norm[ti] = v_op(
                    lambda E, x=xt, o=lnbf[st % 2][:, :], s_=st:
                    E.tensor_scalar(o, x, sA0(MV, s_), sA0(YY, s_),
                                    ALU.add, ALU.mult), inc=True)
                P.wait("tensor", "cV", snap_norm[ti])
                if ti == 0:
                    P.wait("tensor", "qS", qS_pre)
                for grp in range(4):
                    if st == 0 and grp == 0:
                        if c - 2 in snap_lnTmm:
                            P.wait("vector", "cT", snap_lnTmm[c - 2])
                            P.wait("vector", "qO", snap_spill[c - 2])
                    src = lnbf[st % 2]
                    dst = lnT[c % 2].rearrange("p (a t) -> p a t", t=512)[
                        :, 8 * grp:8 * grp + 8, st * 128:(st + 1) * 128]
                    transpose_group(
                        lambda j, s=src, g=grp:
                        s[:, (8 * g + j) * 128:(8 * g + j + 1) * 128], dst)
                snap_tile_tr[ti] = P.n["cT"]
            snap_lnTev[c] = P.n["cV"]

            lnc = lnT[c % 2].rearrange("p (a t) -> p a t", t=512)
            for kk in range(NKV):
                bank = PAB[ab_ctr[0] % 2]
                if ab_ctr[0] - 2 in snap_rope:
                    P.wait("tensor", "cV", snap_rope[ab_ctr[0] - 2])
                P.wait("tensor", "cV", snap_lnTev[c])
                if c == 0 and kk == 0:
                    for d in range(32):
                        if d % 8 == 0:
                            P.wait("tensor", "qG", wk_piece[d // 8])
                        mm(bank,
                           wk_bf[:, d * 256 + kk * 128: d * 256 + (kk + 1) * 128],
                           lnc[:, d, :], d == 0, d == 31, inc=(d == 31))
                    kmm = P.n["cT"]
                else:
                    kmm = mm_group(
                        bank,
                        lambda d, k=kk: wk_bf[:, d * 256 + k * 128: d * 256 + (k + 1) * 128],
                        lambda d: lnc[:, d, :])
                P.wait("vector", "cT", kmm)
                P.wait("vector", "qO", snap_cos[c])
                rope_emit(bank, kT[:, kk * T + 512 * c: kk * T + 512 * (c + 1)])
            snap_ropedone[c] = P.n["cV"]
            for vv in range(NKV):
                bank = PCD[vv]
                if (c - 1, vv) in snap_vcp:
                    P.wait("tensor", "cV", snap_vcp[(c - 1, vv)])
                if c == 0 and vv == 0:
                    for d in range(32):
                        if d % 8 == 0:
                            P.wait("tensor", "qG", wv_piece[d // 8])
                        mm(bank,
                           wv_bf[:, d * 256 + vv * 128: d * 256 + (vv + 1) * 128],
                           lnc[:, d, :], d == 0, d == 31, inc=(d == 31))
                    vmm = P.n["cT"]
                else:
                    vmm = mm_group(
                        bank,
                        lambda d, k=vv: wv_bf[:, d * 256 + k * 128: d * 256 + (k + 1) * 128],
                        lambda d: lnc[:, d, :])
                P.wait("vector", "cT", vmm)
                snap_vcp[(c, vv)] = v_op(
                    lambda E, b=bank, o=vtst[:, vv * 512:(vv + 1) * 512]:
                    E.tensor_copy(out=o, in_=b), inc=True)
            P.wait("tensor", "cV", snap_vcp[(c, 1)])
            transpose_group(
                lambda j: vtst[:, (j % 2) * 512 + (j // 2) * 128:
                               (j % 2) * 512 + (j // 2) * 128 + 128],
                Vsb[:, c * 1024:(c + 1) * 1024].rearrange(
                    "p (a t) -> p a t", t=128))
            snap_lnTmm[c] = P.n["cT"]
            P.wait("scalar", "cV", snap_lnTev[c])
            for a in range(4):
                dma("scalar",
                    lnT_sc.rearrange("(a p) t -> p a t", p=128)[
                        :, 8 * a:8 * a + 8, 512 * c:512 * (c + 1)],
                    lnT[c % 2].rearrange("p (a t) -> p a t", t=512)[
                        :, 8 * a:8 * a + 8, :], "qO")
            snap_spill[c] = P.n["qO"]

        cT_P1 = P.n["cT"]
        cV_P1 = P.n["cV"]
        cS_P1 = P.n["cS"]

        snap_p2mm = {}
        for qp in range(2):
            if qp == 0:
                P.wait("gpsimd", "cT", cT_P1)
            else:
                P.wait("gpsimd", "cT", snap_p2mm[(0, 7)])
            qg_piece = []
            for a in range(4):
                qg_piece.append(dma(
                    "gpsimd",
                    wq_bf.rearrange("p (a c) -> p a c", c=512)[:, 8 * a:8 * a + 8],
                    wqt.rearrange("(a p) c -> p a c", p=128)[
                        :, 8 * a:8 * a + 8, 512 * qp:512 * (qp + 1)], "qG"))
            qG_wq = P.n["qG"]
            for c in range(8):
                if qp == 0 and c < 2:
                    P.wait("sync", "cT", cT_P1)
                    P.wait("sync", "qO", snap_spill[7])
                prev = (qp, c - 2) if c >= 2 else ((qp - 1, 6 + c) if qp else None)
                if prev in snap_p2mm:
                    P.wait("sync", "cT", snap_p2mm[prev])
                for a in range(4):
                    dma("sync",
                        lnT[c % 2].rearrange("p (a t) -> p a t", t=512)[
                            :, 8 * a:8 * a + 8, :],
                        lnT_sc.rearrange("(a p) t -> p a t", p=128)[
                            :, 8 * a:8 * a + 8, 512 * c:512 * (c + 1)], "qS")
                qs_ln = P.n["qS"]
                key_prev = ("p2", qp, c - 1) if c else (
                    ("p2", 0, 7) if qp else 7)
                if key_prev in snap_ropedone:
                    P.wait("scalar", "cV", snap_ropedone[key_prev])
                dma("scalar", cos_sb[:], cost[:, 512 * c:512 * (c + 1)], "qO")
                sc_cos = dma("scalar", sin_sb[:], sint[:, 512 * c:512 * (c + 1)], "qO")
                lnc = lnT[c % 2].rearrange("p (a t) -> p a t", t=512)
                for hh in range(4):
                    H = 4 * qp + hh
                    bank = PAB[ab_ctr[0] % 2]
                    if ab_ctr[0] - 2 in snap_rope:
                        P.wait("tensor", "cV", snap_rope[ab_ctr[0] - 2])
                    P.wait("tensor", "qS", qs_ln)
                    if c == 0 and hh == 0:
                        for d in range(32):
                            if d % 8 == 0:
                                P.wait("tensor", "qG", qg_piece[d // 8])
                            mm(bank,
                               wq_bf[:, d * 512 + hh * 128:
                                     d * 512 + (hh + 1) * 128],
                               lnc[:, d, :], d == 0, d == 31, inc=(d == 31))
                        qmm = P.n["cT"]
                    else:
                        if hh == 0 and c == 0:
                            P.wait("tensor", "qG", qG_wq)
                        qmm = mm_group(
                            bank,
                            lambda d, h=hh: wq_bf[:, d * 512 + h * 128:
                                                  d * 512 + (h + 1) * 128],
                            lambda d: lnc[:, d, :])
                    P.wait("vector", "cT", qmm)
                    P.wait("vector", "qO", sc_cos)
                    if qp == 0 and c == 0 and hh == 0:
                        P.wait("vector", "cT", cT_P1)
                    rope_emit(bank, qT[:, H * T + 512 * c: H * T + 512 * (c + 1)])
                snap_ropedone[("p2", qp, c)] = P.n["cV"]
                snap_p2mm[(qp, c)] = P.n["cT"]

        cT_P2 = P.n["cT"]
        cV_P2 = P.n["cV"]

        if dbg:
            P.wait("sync", "cV", cV_P2)
            dma("sync", d_kT, kT[:, :], "qS")
            dma("sync", d_V, Vsb[:, :], "qS")
            dma("sync", d_qT, qT[:, :], "qS")

        snap_exp = {}
        snap_mask = {}
        snap_ctxmm = {}
        snap_iter = {}
        snap_ctxdma = {}
        kglob = 0
        it_ctr = 0
        h_end_cT = {}
        ag_qO = {}
        RB = pbG[:, :].bitcast(F32)

        SC3 = [pbA[:, :], pbB[:, :], pbH[:, :].bitcast(F32)]
        pending_outer = [None]
        for h in range(NKV):
            for s in range(8):
                for g in range(4):
                    H = 4 * h + g
                    nk = 4 * s + 4
                    su = PEF[it_ctr % 2]
                    cx = PCD[it_ctr % 2]
                    if it_ctr - 2 in snap_iter:
                        P.wait("tensor", "cV", snap_iter[it_ctr - 2])
                    base = kglob

                    def emit_scores(j, h=h, H=H, s=s):
                        jg = base + j
                        sb_ = SC3[jg % 3]
                        eb = EB[jg % 4]
                        if jg - 3 in snap_exp:
                            P.wait("tensor", "cS", snap_exp[jg - 3])
                        if jg == 0:
                            P.wait("tensor", "cV", cV_P2)
                        smm = mm(sb_,
                                 kT[:, h * T + 128 * j: h * T + 128 * (j + 1)],
                                 qT[:, H * T + 512 * s: H * T + 512 * (s + 1)],
                                 True, True, inc=True)
                        if (j == 0 and pending_outer[0] is not None
                                and pending_outer[0][0] == "j0"):
                            pending_outer[0][1]()
                            pending_outer[0] = None
                        if jg - 4 in snap_ctxmm:
                            P.wait("scalar", "cT", snap_ctxmm[jg - 4])
                        P.wait("scalar", "cT", smm)
                        if jg == 0:
                            P.wait("scalar", "cV", cV_P2)
                        snap_exp[jg] = s_op(
                            lambda E, p=sb_, o=eb: E.activation(
                                o, p, AF.Exp, scale=ISQ), inc=True)
                        if j >= 4 * s:
                            off = 384 - (128 * j - 512 * s)
                            P.wait("vector", "cS", snap_exp[jg])
                            snap_mask[jg] = v_op(
                                lambda E, o=eb, m=mask_sb[:, off:off + 512]:
                                E.tensor_tensor(out=o, in0=o, in1=m,
                                                op=ALU.mult), inc=True)

                    emit_scores(0)
                    if nk > 1:
                        emit_scores(1)
                    for k in range(nk):
                        if k + 2 < nk:
                            emit_scores(k + 2)
                        kg = base + k
                        eb = EB[kg % 4]
                        if kg in snap_mask:
                            P.wait("tensor", "cV", snap_mask[kg])
                        else:
                            P.wait("tensor", "cS", snap_exp[kg])
                        mm(su, ones_bf[:, :], eb, k == 0, k == nk - 1)
                        snap_ctxmm[kg] = mm(
                            cx, Vsb[:, k * 256 + h * 128: k * 256 + (h + 1) * 128],
                            eb, k == 0, k == nk - 1, inc=True)
                        if (k == 1 and pending_outer[0] is not None
                                and pending_outer[0][0] == "k1"):
                            pending_outer[0][1]()
                            pending_outer[0] = None
                    kglob = base + nk
                    P.wait("vector", "cT", snap_ctxmm[kglob - 1])
                    rec = ropeB[0:1, :]
                    vrec = v_op(lambda E, s_=su: E.reciprocal(rec, s_[0:1, :]),
                                inc=True)
                    if it_ctr == 63:
                        P.wait("tensor", "cV", vrec)
                        romm = t_op(lambda E: E.matmul(
                            RB, ones_f[:], rec, start=True, stop=True,
                            skip_group_check=True), inc=True)
                    else:
                        nxt_s = ((it_ctr + 1) % 32) // 4
                        if nxt_s == 0:
                            mode, skewinc = "j0", 2
                        else:
                            mode, skewinc = "k1", 7
                        romm = P.n["cT"] + skewinc

                        def do_outer(vr=vrec, expect=romm):
                            P.wait("tensor", "cV", vr)
                            got = t_op(lambda E: E.matmul(
                                RB, ones_f[:], rec, start=True, stop=True,
                                skip_group_check=True), inc=True)
                            assert got == expect, (got, expect)
                        pending_outer[0] = (mode, do_outer)
                    P.wait("vector", "cT", romm)
                    v_op(lambda E: E.tensor_copy(out=ropeA[:], in_=RB))
                    if it_ctr - 2 in snap_ctxdma:
                        P.wait("vector", "qS", snap_ctxdma[it_ctr - 2])
                    cst = ctxst[it_ctr % 2]
                    vdone = v_op(lambda E, c_=cx, o=cst: E.tensor_tensor(
                        out=o, in0=c_, in1=ropeA[:], op=ALU.mult), inc=True)
                    snap_iter[it_ctr] = vdone
                    P.wait("sync", "cV", vdone)
                    snap_ctxdma[it_ctr] = dma(
                        "sync",
                        cc_in[h][s // 2][128 * g:128 * (g + 1),
                                         512 * (s % 2):512 * (s % 2 + 1)],
                        cst, "qS")
                    it_ctr += 1
                if s % 2 == 1:
                    ag_qO[(h, s // 2)] = P.n["qS"]
            h_end_cT[h] = P.n["cT"]
        assert pending_outer[0] is None

        def emit_ag(h, qt):
            P.wait("gpsimd", "qS", ag_qO[(h, qt)])
            v = P.bump("qG", 1)
            P.emit("gpsimd", lambda E, hh=h, q=qt: E.collective_compute(
                "AllGather", ALU.bypass,
                replica_groups=[[0, 1, 2, 3], [4, 5, 6, 7]],
                ins=[cc_in[hh][q]],
                outs=[cc_out[hh][q]]).then_inc(E._sems["qG"], 1))
            return v

        def emit_switch_loads(h):
            for _ in range(16):
                P.bump("qG", 16)

            def do_switch(E, hh=h):
                pid = E.partition_id()
                qv = E.compute_val(pid % 4)
                for body in nc.Switch(engines=[E], index=[qv], n=4):
                    for blk in range(16):
                        E.dma_start(
                            out=ctxF[:, (hh * 16 + blk) * 1024:
                                     (hh * 16 + blk + 1) * 1024],
                            in_=cc_out[hh][body][blk * 128:(blk + 1) * 128, :],
                        ).then_inc(E._sems["qG"], 16)
            P.emit("gpsimd", do_switch)
            return P.n["qG"]

        def emit_wo_cast(e_, wait_cT):
            P.wait("gpsimd", "cT", wait_cT)
            for a in range(4):
                dma("gpsimd",
                    wo_bf[e_ % 2].rearrange("p (a c) -> p a c", c=512)[
                        :, 8 * a:8 * a + 8],
                    wot.rearrange("(a p) c -> p a c", p=128)[
                        :, 8 * a:8 * a + 8, 512 * e_:512 * (e_ + 1)], "qG")
            return P.n["qG"]

        for qt in range(4):
            emit_ag(0, qt)
        ag0_last = P.n["qG"]
        P.wait("gpsimd", "cT", cT_P2)
        P.wait("gpsimd", "qG", ag0_last)
        qG_ld0 = emit_switch_loads(0)
        emit_ag(1, 0)
        emit_ag(1, 1)
        snap_wocast = {0: emit_wo_cast(0, h_end_cT[0])}
        emit_ag(1, 2)
        emit_ag(1, 3)
        P.wait("gpsimd", "qG", P.n["qG"])
        qG_ld1 = emit_switch_loads(1)
        snap_wocast[1] = emit_wo_cast(1, h_end_cT[1])

        snap_p5mm = {}
        snap_evac5 = {}
        snap_x2dma = {}
        ctxFv = ctxF.rearrange("p (a t) -> p a t", t=1024)

        if dbg:
            P.wait("sync", "qG", qG_ld1)
            dma("sync", d_ctxF, ctxF[:, :], "qS")
            P.wait("sync", "qG", snap_wocast[0])
            dma("sync", d_wo0, wo_bf[0], "qS")

        xeC = kT_f32[:, :].rearrange("p (a t) -> p a t", t=512)
        snap_xeC = {}
        tt_ctr = 0
        for e_ in range(8):
            wob = wo_bf[e_ % 2].rearrange("p (a c) -> p a c", c=512)
            if e_ == 0:
                P.wait("sync", "cT", h_end_cT[1])
            else:
                P.wait("sync", "cV", snap_evac5[8 * e_ - 1])
            snap_xeC[e_] = dma(
                "sync", xeC,
                xq.rearrange("(a p) d -> p a d", p=128)[:, :,
                                                        512 * e_:512 * (e_ + 1)],
                "qS")
            for tt in range(8):
                bank = PAB[tt_ctr % 2]
                if tt_ctr - 2 in snap_evac5:
                    P.wait("tensor", "cV", snap_evac5[tt_ctr - 2])
                if tt == 0:
                    P.wait("tensor", "qG", snap_wocast[e_])
                    if e_ == 0:
                        P.wait("tensor", "qG", qG_ld1)
                xet = xeC[:, tt, :]
                qs_xe = snap_xeC[e_]
                pm = mm_group(bank,
                              lambda d, t_=tt: ctxFv[:, d, t_ * 128:(t_ + 1) * 128],
                              lambda d: wob[:, d, :])
                snap_p5mm[(e_, tt)] = pm
                if dbg and e_ == 0 and tt == 0:
                    P.wait("vector", "cT", pm)
                    vm5 = v_op(lambda E, p=bank: E.tensor_copy(
                        out=ropeA[:], in_=p), inc=True)
                    P.wait("scalar", "cV", vm5)
                    qp5 = dma("scalar", d_op, ropeA[:], "qO")
                    P.wait("vector", "qO", qp5)
                P.wait("vector", "cT", pm)
                P.wait("vector", "qS", qs_xe)
                if tt_ctr - 2 in snap_x2dma:
                    P.wait("vector", "qO", snap_x2dma[tt_ctr - 2])
                x2s = x2st[tt_ctr % 2]
                c1 = p5st[:, tt * 16 + e_: tt * 16 + e_ + 1]
                c2 = p5st[:, tt * 16 + 8 + e_: tt * 16 + 8 + e_ + 1]
                v_op(lambda E, b=bank, x=xet, o=x2s, a=c1: E.scalar_tensor_tensor(
                    o, b, 1.0, x, ALU.mult, ALU.add, accum_out=a))
                j5 = Vsb[:, 2048:2560]
                snap_evac5[tt_ctr] = v_op(
                    lambda E, o=j5, x=x2s, a=c2: E.scalar_tensor_tensor(
                        o, x, 1.0, x, ALU.mult, ALU.mult, accum_out=a), inc=True)
                P.wait("scalar", "cV", snap_evac5[tt_ctr])
                snap_x2dma[tt_ctr] = dma(
                    "scalar",
                    x2_sc[128 * tt:128 * (tt + 1), 512 * e_:512 * (e_ + 1)],
                    x2s, "qO")
                tt_ctr += 1
            if e_ + 2 <= 7:
                snap_wocast[e_ + 2] = emit_wo_cast(e_ + 2, snap_p5mm[(e_, 7)])

        cV_P5 = P.n["cV"]
        qO_P5 = P.n["qO"]

        snap_n2 = {}
        snap_tr2t = {}
        for tt in range(8):
            v_op(lambda E, o=sA0(S1, tt), i=p5st[:, tt * 16:tt * 16 + 8]:
                 E.reduce_sum(o, i, axis=AX.X))
            v_op(lambda E, o=sA0(S2, tt), i=p5st[:, tt * 16 + 8:tt * 16 + 16]:
                 E.reduce_sum(o, i, axis=AX.X))
        ln_chain(8, kT_f32[:, 2048:2560], kT_f32[:, 2560:3072], iters=5)
        for tt in range(8):
            par = tt % 2
            if tt - 2 in snap_n2:
                P.wait("sync", "cV", snap_n2[tt - 2])
            if tt == 0:
                P.wait("sync", "qO", qO_P5)
            qs_x2 = dma("sync", x2row[par][:, :],
                        x2_sc[128 * tt:128 * (tt + 1), :], "qS")
            P.wait("vector", "qS", qs_x2)
            if tt - 2 in snap_tr2t:
                P.wait("vector", "cT", snap_tr2t[tt - 2])
            snap_n2[tt] = v_op(
                lambda E, x=x2row[par][:, :], o=ln2bf[par][:, :], t_=tt:
                E.tensor_scalar(o, x, sA0(MV, t_), sA0(YY, t_),
                                ALU.add, ALU.mult), inc=True)
            P.wait("tensor", "cV", snap_n2[tt])
            if tt == 0:
                P.wait("tensor", "cV", cV_P5)
            for grp in range(4):
                src = ln2bf[par]
                dst = ln2T.rearrange("p (a t) -> p a t", t=1024)[
                    :, 8 * grp:8 * grp + 8, tt * 128:(tt + 1) * 128]
                transpose_group(
                    lambda j, s=src, g=grp:
                    s[:, (8 * g + j) * 128:(8 * g + j + 1) * 128], dst)
            snap_tr2t[tt] = P.n["cT"]
        cV_ln2 = P.n["cV"]

        snap_itcast = {}
        snap_f6mm = {}
        snap_yev = {}
        snap_ydma = {}

        def emit_it_cast(e_, wait_cT):
            P.wait("gpsimd", "cT", wait_cT)
            for a in range(4):
                dma("gpsimd",
                    it_bf[e_ % 2].rearrange("p (a c) -> p a c", c=512)[
                        :, 8 * a:8 * a + 8],
                    itt.rearrange("(a p) c -> p a c", p=128)[
                        :, 8 * a:8 * a + 8, 512 * e_:512 * (e_ + 1)], "qG")
            return P.n["qG"]

        snap_itcast[0] = emit_it_cast(0, snap_p5mm[(6, 7)])
        snap_itcast[1] = emit_it_cast(1, snap_p5mm[(7, 7)])

        ln2Tv = ln2T.rearrange("p (a t) -> p a t", t=1024)
        tt6 = 0
        for e_ in range(8):
            itb = it_bf[e_ % 2].rearrange("p (a c) -> p a c", c=512)
            for tt in range(8):
                bank = PAB[tt6 % 2]
                if tt6 - 2 in snap_yev:
                    P.wait("tensor", "cV", snap_yev[tt6 - 2])
                if tt == 0:
                    P.wait("tensor", "qG", snap_itcast[e_])
                    if e_ == 0:
                        P.wait("tensor", "cV", cV_ln2)
                xet2 = xe2[tt6 % 2]
                if tt6 - 2 in snap_yev:
                    P.wait("sync", "cV", snap_yev[tt6 - 2])
                qs_xe2 = dma(
                    "sync", xet2,
                    x2_sc[128 * tt:128 * (tt + 1), 512 * e_:512 * (e_ + 1)], "qS")
                fm = mm_group(bank,
                              lambda d, t_=tt: ln2Tv[:, d, t_ * 128:(t_ + 1) * 128],
                              lambda d: itb[:, d, :])
                snap_f6mm[(e_, tt)] = fm
                P.wait("vector", "cT", fm)
                P.wait("vector", "qS", qs_xe2)
                if tt6 - 2 in snap_ydma:
                    P.wait("vector", "qO", snap_ydma[tt6 - 2])
                yb = yst[tt6 % 2]
                snap_yev[tt6] = v_op(
                    lambda E, b=bank, x=xet2, o=yb: E.tensor_add(o, b, x), inc=True)
                P.wait("scalar", "cV", snap_yev[tt6])
                snap_ydma[tt6] = dma(
                    "scalar", y[128 * tt:128 * (tt + 1), 512 * e_:512 * (e_ + 1)],
                    yb, "qO")
                tt6 += 1
            if e_ + 2 <= 7:
                snap_itcast[e_ + 2] = emit_it_cast(e_ + 2, snap_f6mm[(e_, 7)])

        if dbg:
            P.wait("sync", "qG", P.n["qG"])
            P.wait("sync", "qO", P.n["qO"])
            for h in range(NKV):
                for qt in range(4):
                    dma("sync", d_cc[h][:, 1024 * qt:1024 * (qt + 1)],
                        cc_out[h][qt], "qS")
            dma("sync", d_lnT, lnT_sc, "qS")
            dma("sync", d_x2, x2_sc, "qS")
            dma("sync", d_acc[:, 0:16], p5st[:, 0:16], "qS")

        P.wait("sync", "qO", P.n["qO"])
        P.wait("sync", "qG", P.n["qG"])
        P.wait("sync", "qS", P.n["qS"])

        with nc.Block() as block:

            def run(E, eng):
                E._sems = sems
                for fn in P.ops[eng]:
                    fn(E)

            @block.sync
            def _(E):
                run(E, "sync")

            @block.gpsimd
            def _(E):
                run(E, "gpsimd")

            @block.scalar
            def _(E):
                run(E, "scalar")

            @block.vector
            def _(E):
                run(E, "vector")

            @block.tensor
            def _(E):
                run(E, "tensor")

    return nc


def _rope_tables():
    ROPE_THETA = 500000.0
    ROPE_FACTOR = 32.0
    HI, LO = 4.0, 1.0
    ORIG = 8192
    d = np.arange(0, DH, 2, dtype=np.float64) / DH
    inv = 1.0 / ROPE_THETA ** d
    wav = 2.0 * math.pi / inv
    cut_lo = ORIG / HI
    cut_hi = ORIG / LO
    inv = np.where(wav > cut_hi, inv / ROPE_FACTOR, inv)
    smooth = (ORIG / wav - LO) / (HI - LO)
    smoothed = (1.0 - smooth) * inv / ROPE_FACTOR + smooth * inv
    med = (~(wav < cut_hi)) & (~(wav > cut_lo))
    inv = np.where(med, smoothed, inv)
    pos = np.arange(T, dtype=np.float64)
    freqs = pos[:, None] * inv[None, :]
    emb = np.concatenate([freqs, freqs], axis=1)
    cos = np.ascontiguousarray(np.cos(emb).T.astype(np.float32))
    sin = np.sin(emb).T.astype(np.float32)
    sin_signed = sin.copy()
    sin_signed[0:64] = -sin[0:64]
    return cos, np.ascontiguousarray(sin_signed)


def kernel(L0_attn_x, L0_attn_Wq, L0_attn_Wk, L0_attn_Wv, L0_attn_Wo,
           L0_ffn_identity):
    x = np.asarray(L0_attn_x, dtype=np.float32)
    Wq = np.asarray(L0_attn_Wq, dtype=np.float32)
    Wk = np.asarray(L0_attn_Wk, dtype=np.float32)
    Wv = np.asarray(L0_attn_Wv, dtype=np.float32)
    Wo = np.asarray(L0_attn_Wo, dtype=np.float32)
    I = np.asarray(L0_ffn_identity, dtype=np.float32)

    cosT, sinT = _rope_tables()
    mask = np.zeros((128, 896), dtype=ml_dtypes.bfloat16)
    for p in range(128):
        mask[p, p + 384:] = 1.0
    ident = np.eye(128, dtype=ml_dtypes.bfloat16)
    p64 = np.zeros((128, 128), dtype=ml_dtypes.bfloat16)
    for i in range(128):
        p64[i, i ^ 64] = 1.0

    perm = []
    for h in range(2):
        for r in range(4):
            for g in range(4):
                head = 8 * r + 4 * h + g
                perm.extend(range(head * 128, head * 128 + 128))
    WoT_perm = np.ascontiguousarray(Wo.T[np.array(perm)])
    IT = np.ascontiguousarray(I.T)

    in_maps = []
    for c in range(8):
        b, qq = c // 4, c % 4
        in_maps.append({
            "xb": np.ascontiguousarray(x[b]),
            "xq": np.ascontiguousarray(x[b, 1024 * qq:1024 * (qq + 1)]),
            "wqt": np.ascontiguousarray(Wq[1024 * qq:1024 * (qq + 1), :].T),
            "wkt": np.ascontiguousarray(Wk[256 * qq:256 * (qq + 1), :].T),
            "wvt": np.ascontiguousarray(Wv[256 * qq:256 * (qq + 1), :].T),
            "wot": WoT_perm,
            "itt": IT,
            "cost": cosT,
            "sint": sinT,
            "maskt": mask,
            "identt": ident,
            "p64t": p64,
        })

    nc = build_nc()
    res = run_bass_kernel_spmd(nc, in_maps, core_ids=list(range(8)))
    out = np.empty((2, T, D), dtype=np.float32)
    for c in range(8):
        b, qq = c // 4, c % 4
        out[b, 1024 * qq:1024 * (qq + 1)] = res.results[c]["y"]
    return out
